# version 16
# baseline (speedup 1.0000x reference)
import contextlib

import numpy as np
import ml_dtypes
import concourse.bass as bass
import concourse.mybir as mybir
from concourse.bass_utils import run_bass_kernel_spmd

F32 = mybir.dt.float32
BF16 = mybir.dt.bfloat16
AF = mybir.ActivationFunctionType
ALU = mybir.AluOpType
AX = mybir.AxisListType

NCORES = 8
H, D, M, DM, HID, NE = 16, 64, 256, 1024, 2048, 8
KT = DM // 128
LNE1 = 1e-5
LNE2 = 1e-6
ST_W = H * (2 * D + 2)


class _Haz:
    __slots__ = ("w", "rs")

    def __init__(self):
        self.w = None
        self.rs = {}

    def rd_waits(self):
        return [self.w] if self.w else []

    def wr_waits(self):
        ws = dict(self.rs)
        if self.w:
            ws[self.w[0]] = max(ws.get(self.w[0], 0), self.w[1])
        return list(ws.items())

    def note_rd(self, sem, val):
        self.rs[sem] = max(self.rs.get(sem, 0), val)

    def note_wr(self, sem, val):
        self.w = (sem, val)
        self.rs = {}


class Prog:
    ENGS = ("t", "v", "s", "g", "y")

    def __init__(self, nc):
        self.nc = nc
        self.ops = {e: [] for e in self.ENGS}
        self.phases = {e: [] for e in self.ENGS}
        self.phase = "P0"
        self.sems = {}
        self.ctr = {}
        self.haz = {}
        self.known = {e: {} for e in self.ENGS}
        self.batch_open = {}

    def sem(self, name):
        self.sems[name] = self.nc.alloc_semaphore(name)
        self.ctr[name] = 0
        return name

    def val(self, name):
        return self.ctr[name]

    def hz(self, key):
        if key not in self.haz:
            self.haz[key] = _Haz()
        return self.haz[key]

    def barrier(self):
        snap = [(sn, v) for sn, v in self.ctr.items() if v > 0]
        for eng in self.ENGS:
            kn = self.known[eng]
            final = [(sn, v) for sn, v in snap if kn.get(sn, -1) < v]
            for sn, v in final:
                kn[sn] = v
            if not final:
                continue
            sems = self.sems

            def run(e, final=tuple(final)):
                for sn, v in final:
                    e.wait_ge(sems[sn], v)
                e.nop()
            self.ops[eng].append(run)
            self.phases[eng].append(self.phase)

    def op(self, eng, fn, inc, amt=1, rd=(), wr=(), wrpre=(), wrpost=(), waits=(),
           gate=None):
        wl = {}
        for sn, v in waits:
            if v > 0:
                wl[sn] = max(wl.get(sn, 0), v)
        for k in rd:
            for sn, v in self.hz(k).rd_waits():
                wl[sn] = max(wl.get(sn, 0), v)
        for k in tuple(wr) + tuple(wrpre):
            for sn, v in self.hz(k).wr_waits():
                wl[sn] = max(wl.get(sn, 0), v)
        if amt == 16 and inc is not None:
            opn = self.batch_open.get(inc, False)
            g_ = gate
            if g_ is None:
                if inc in ("dX", "dX1", "dW", "dC", "dC2", "dP", "dP2", "dL", "dL2"):
                    g_ = bool(wrpre or wr) or not opn
                else:
                    g_ = True
            if g_ and self.ctr[inc] > 0:
                wl[inc] = max(wl.get(inc, 0), self.ctr[inc])
            if g_:
                opn = True
            if wrpost or wr:
                opn = False
            self.batch_open[inc] = opn
        kn = self.known[eng]
        final = [(sn, v) for sn, v in wl.items() if kn.get(sn, -1) < v]
        for sn, v in final:
            kn[sn] = v
        assert inc is not None or (not rd and not wr and not wrpre and not wrpost), \
            "tracked op needs inc"
        if inc is not None:
            self.ctr[inc] += amt
            myv = self.ctr[inc]
            for k in rd:
                self.hz(k).note_rd(inc, myv)
            for k in tuple(wr) + tuple(wrpost):
                self.hz(k).note_wr(inc, myv)
        sems = self.sems

        def run(e, fn=fn, final=tuple(final), inc=inc, amt=amt):
            for sn, v in final:
                e.wait_ge(sems[sn], v)
            ins = fn(e)
            if inc is not None:
                ins.then_inc(sems[inc], amt)
        self.ops[eng].append(run)
        self.phases[eng].append(self.phase)
        return self.ctr[inc] if inc is not None else None


class BankPool:
    def __init__(self, slots):
        self.slots = [dict(h=h, free=None, key=f"_bank{i}") for i, h in enumerate(slots)]
        self.i = 0

    def acquire(self):
        s = self.slots[self.i % len(self.slots)]
        self.i += 1
        return s


def fill(p, pool, mms, rd=(), waits=()):
    s = pool.acquire()
    n = len(mms)
    tval = None
    for i, mm in enumerate(mms):
        if i == 0:
            w = list(waits)
            if s["free"] is not None:
                w.append(s["free"])
            for k in rd:
                for sn, v in p.hz(k).rd_waits():
                    w.append((sn, v))
            tval = p.op("t", (lambda e, mm=mm, h=s["h"]: mm(e, h)),
                        inc="sT" if n == 1 else None, waits=w)
        else:
            tval = p.op("t", (lambda e, mm=mm, h=s["h"]: mm(e, h)),
                        inc="sT" if i == n - 1 else None)
    for k in rd:
        p.hz(k).note_rd("sT", tval)
    s["tval"] = tval
    return s, tval


def fill_multi(p, pool, nslots, mms, rd=(), waits=()):
    slots = [pool.acquire() for _ in range(nslots)]
    n = len(mms)
    tval = None
    for i, mm in enumerate(mms):
        if i == 0:
            w = list(waits)
            for s in slots:
                if s["free"] is not None:
                    w.append(s["free"])
            for k in rd:
                for sn, v in p.hz(k).rd_waits():
                    w.append((sn, v))
            tval = p.op("t", (lambda e, mm=mm, hs=tuple(s["h"] for s in slots): mm(e, hs)),
                        inc="sT" if n == 1 else None, waits=w)
        else:
            tval = p.op("t", (lambda e, mm=mm, hs=tuple(s["h"] for s in slots): mm(e, hs)),
                        inc="sT" if i == n - 1 else None)
    for k in rd:
        p.hz(k).note_rd("sT", tval)
    for s in slots:
        s["tval"] = tval
    return slots, tval


def evac(p, eng, slot, tval, fn, rd=(), wr=(), wrpre=(), wrpost=(), waits=()):
    sem = {"v": "sV", "s": "sS"}[eng]
    w = [("sT", tval)] + list(waits)
    val = p.op(eng, (lambda e, fn=fn, h=slot["h"]: fn(e, h)),
               inc=sem, rd=rd, wr=wr, wrpre=wrpre, wrpost=wrpost, waits=w)
    slot["free"] = (sem, val)
    return val


def bf16v(h, col0, ncols):
    return h[:, col0:col0 + ncols // 2].bitcast(BF16)



def build_program(tpc=4096, spg=4):
    nc = bass.Bass("TRN2", target_bir_lowering=False, debug=False)
    SUBS = tpc // 128
    GRP = spg * 128
    NG = tpc // GRP
    MSUB = min(8, SUBS)
    MG = SUBS // MSUB
    MTOK = MSUB * 128

    ap = {}

    def din(name, shape, dt=F32):
        ap[name] = nc.dram_tensor(name, list(shape), dt, kind="ExternalInput").ap()
        return ap[name]

    x0d = din("x0", (tpc, DM))
    for w in ("qw", "kw", "vw", "ow"):
        din(w, (DM, DM))
    for b in ("qb", "kb", "vb", "ob"):
        din(b, (DM,))
    din("proj", (M, D))
    din("gamma", (DM,)); din("beta", (DM,))
    din("Wr", (DM, NE)); din("brr", (NE,)); din("b3", (NE, DM))
    din("w1e", (DM, HID)); din("w2e", (HID, HID))
    din("b1e", (HID,)); din("b2e", (HID,)); din("w3e", (HID, DM))
    din("pmask", (NCORES * 128,))
    outd = nc.dram_tensor("out", [tpc, DM], F32, kind="ExternalOutput").ap()

    ccin_s = nc.dram_tensor("ccin_s", [128, ST_W], F32, kind="Internal")
    ccout_s = nc.dram_tensor("ccout_s", [NCORES * 128, ST_W], F32, kind="Internal",
                             addr_space="Shared")
    WROWS = DM + 1 + HID // 2
    ccin_w = nc.dram_tensor("ccin_w", [WROWS, HID], BF16, kind="Internal")
    ccout_w = nc.dram_tensor("ccout_w", [NCORES * WROWS, HID], BF16, kind="Internal",
                             addr_space="Shared")
    x1scr = nc.dram_tensor("x1scr", [tpc, DM], F32, kind="Internal")
    KC_EQP, KC_KPT, KC_VTM = 0, 4096, 8192
    kcache = nc.dram_tensor("kcache", [tpc, 9216], BF16, kind="Internal")
    lncache = nc.dram_tensor("lncache", [tpc, 2], F32, kind="Internal")

    c_identb = nc.inline_tensor(np.eye(128, dtype=ml_dtypes.bfloat16), "c_identb")
    c_mask = nc.inline_tensor(np.triu(np.ones((128, 128), ml_dtypes.bfloat16)), "c_mask")
    c_onesb = nc.inline_tensor(np.ones((1, 128), ml_dtypes.bfloat16), "c_onesb")
    c_onescb = nc.inline_tensor(np.ones((128, 1), ml_dtypes.bfloat16), "c_onescb")
    c_onesf = nc.inline_tensor(np.ones((1, 128), np.float32), "c_onesf")
    olh = np.zeros((128, 2), np.float32)
    olh[0:64, 0] = 1.0
    olh[64:128, 1] = 1.0
    c_oneslh = nc.inline_tensor(olh, "c_oneslh")
    ckc_np = np.zeros((128, 2), np.float32)
    ckc_np[:, 0] = -1.0 / 128.0
    ckc_np[:, 1] = -np.log(16.0)
    c_ckc = nc.inline_tensor(ckc_np, "c_ckc")

    r2a = contextlib.ExitStack()
    r0 = contextlib.ExitStack()
    r1 = contextlib.ExitStack()
    r2b = contextlib.ExitStack()
    r3 = contextlib.ExitStack()

    def mk(stack, name, shape, dt):
        return stack.enter_context(nc.sbuf_tensor(name, shape, dt))

    class _Res:
        pass
    identb = nc.alloc_sbuf_tensor("identb", [128, 128], BF16)
    masktri = nc.alloc_sbuf_tensor("masktri", [128, 128], BF16)
    onesb = nc.alloc_sbuf_tensor("onesb", [1, 128], BF16)
    onescb = nc.alloc_sbuf_tensor("onescb", [128, 1], BF16)
    onesf = nc.alloc_sbuf_tensor("onesf", [1, 128], F32)
    brb = nc.alloc_sbuf_tensor("brb", [1, NE], BF16)
    b3b = nc.alloc_sbuf_tensor("b3b", [NE, DM], BF16)
    wrb = nc.alloc_sbuf_tensor("wrb", [128, KT * NE], BF16)
    b12f = nc.alloc_sbuf_tensor("b12f", [128, NE * 16], F32)
    pmv = nc.alloc_sbuf_tensor("pmv", [128, NCORES], F32)
    stat = nc.alloc_sbuf_tensor("stat", [128, 80], F32)
    ckcst = nc.alloc_sbuf_tensor("ckcst", [128, 2], F32)
    sqscr = nc.alloc_sbuf_tensor("sqscr", [128, 512], F32)
    sqscr2 = nc.alloc_sbuf_tensor("sqscr2", [128, 512], F32)

    wqb = mk(r2a, "wqb", [128, KT * DM], BF16)
    wkb = mk(r2a, "wkb", [128, KT * DM], BF16)
    wvb = mk(r2a, "wvb", [128, KT * DM], BF16)
    wob = mk(r2a, "wob", [128, KT * DM], BF16)
    projLo = mk(r2a, "projLo", [128, M], BF16)
    projHi = mk(r2a, "projHi", [128, M], BF16)
    qbt = mk(r2a, "qbt", [128, KT], F32)
    kbt = mk(r2a, "kbt", [128, KT], F32)
    oneslh = mk(r2a, "oneslh", [128, 2], F32)
    kbr = mk(r2a, "kbr", [1, DM], BF16)
    vbr = mk(r2a, "vbr", [1, DM], BF16)
    obr = mk(r2a, "obr", [1, DM], BF16)
    grow = mk(r2a, "grow", [1, DM], F32)
    brow = mk(r2a, "brow", [1, DM], F32)

    pool = BankPool([nc.alloc_psum_tensor(f"ps{i}", [128, 512], F32) for i in range(8)])

    p = Prog(nc)
    for s in ("sT", "sV", "sS", "sG", "dC", "dC2", "dP", "dX", "dW", "dX1", "dSt",
              "dO", "gC", "dG", "dK", "dL", "dL2", "dP2"):
        p.sem(s)

    def ldc(dst, src, wr, rd=(), slow=False):
        return p.op("y", lambda e, d=dst, s=src, slow=slow: e.dma_start(
            out=d, in_=s, allow_slow_non_contiguous=slow),
            inc="dC", amt=16, wr=wr, rd=rd)

    ldc(identb[:, :], c_identb.ap(), wr=("identb",))
    ldc(masktri[:, :], c_mask.ap(), wr=("masktri",))
    ldc(onesb[:, :], c_onesb.ap(), wr=("onesb",))
    ldc(onescb[:, :], c_onescb.ap(), wr=("onescb",))
    ldc(onesf[:, :], c_onesf.ap(), wr=("onesf",))
    ldc(pmv[:, :], ap["pmask"].rearrange("(r p) -> p r", p=128), wr=("pmv",), slow=True)
    ldc(qbt[:, :], ap["qb"].rearrange("(k p) -> p k", p=128), wr=("qbt",), slow=True)
    ldc(kbt[:, :], ap["kb"].rearrange("(k p) -> p k", p=128), wr=("kbt",), slow=True)
    ldc(oneslh[:, :], c_oneslh.ap(), wr=("oneslh",))
    ldc(ckcst[:, :], c_ckc.ap(), wr=("ckcst",))

    stg_w = [mk(r0, f"stg_w{i}", [128, KT * DM], F32) for i in range(2)]
    stg_r = mk(r0, "stg_r", [1, DM], F32)
    stg_p = mk(r0, "stg_p", [128, 2 * D], F32)
    stg_pb = mk(r0, "stg_pb", [128, 2 * D], BF16)
    stg_wr = mk(r0, "stg_wr", [128, KT * NE], F32)
    stg_b3 = mk(r0, "stg_b3", [NE, DM], F32)
    stg_g = mk(r0, "stg_g", [1, DM], F32)

    for wi, (name, dstb, key) in enumerate((("qw", wqb, "wqb"), ("kw", wkb, "wkb"),
                                            ("vw", wvb, "wvb"), ("ow", wob, "wob"))):
        par_ = wi % 2
        cls = "dC" if par_ == 0 else "dC2"
        for k in range(KT):
            kw = dict(wrpre=(f"stg_w{par_}",)) if k == 0 else {}
            if k == KT - 1:
                kw["wrpost"] = (f"stg_w{par_}",)
            p.op("y", lambda e, name=name, k=k, par_=par_: e.dma_start(
                out=stg_w[par_][:, k * DM:(k + 1) * DM],
                in_=ap[name][k * 128:(k + 1) * 128, :]),
                inc=cls, amt=16, **kw)
        p.op("v", lambda e, d=dstb, par_=par_: e.tensor_copy(d[:, :], stg_w[par_][:, :]),
             inc="sV", rd=(f"stg_w{par_}",), wr=(key,))
    for name, dst, key in (("kb", kbr, "kbr"), ("vb", vbr, "vbr"), ("ob", obr, "obr")):
        ldc(stg_r[:, :], ap[name].rearrange("(one f) -> one f", one=1), wr=("stg_r",))
        p.op("v", lambda e, d=dst: e.tensor_copy(d[:, :], stg_r[:, :]),
             inc="sV", rd=("stg_r",), wr=(key,))
    ldc(stg_r[:, 0:NE], ap["brr"].rearrange("(one f) -> one f", one=1), wr=("stg_r",))
    p.op("v", lambda e: e.tensor_copy(brb[:, :], stg_r[:, 0:NE]),
         inc="sV", rd=("stg_r",), wr=("brb",))
    ldc(stg_b3[:, :], ap["b3"], wr=("stg_b3",))
    p.op("v", lambda e: e.tensor_copy(b3b[:, :], stg_b3[:, :]),
         inc="sV", rd=("stg_b3",), wr=("b3b",))
    for k in range(KT):
        kw = dict(wrpre=("stg_wr",)) if k == 0 else {}
        if k == KT - 1:
            kw["wrpost"] = ("stg_wr",)
        p.op("y", lambda e, k=k: e.dma_start(
            out=stg_wr[:, k * NE:(k + 1) * NE], in_=ap["Wr"][k * 128:(k + 1) * 128, :]),
            inc="dC", amt=16, **kw)
    p.op("v", lambda e: e.tensor_copy(wrb[:, :], stg_wr[:, :]),
         inc="sV", rd=("stg_wr",), wr=("wrb",))
    ldc(stg_p[:, 0:D], ap["proj"][0:128, :], wr=())
    ldc(stg_p[:, D:2 * D], ap["proj"][128:256, :], wr=("stg_p",))
    p.haz["stg_p"].w = ("dC", p.val("dC"))
    p.op("v", lambda e: e.tensor_scalar(stg_pb[:, :], stg_p[:, :], 0.125, None, ALU.mult),
         inc="sV", rd=("stg_p",), wr=("stg_pb",), waits=[("dC", p.val("dC"))])
    p.op("v", lambda e: e.memset(projLo[64:128, :], 0.0), inc="sV", wrpre=("projLo",))
    pmz = p.op("v", lambda e: e.memset(projHi[0:64, :], 0.0), inc="sV", wrpre=("projHi",))
    for j in range(2):
        sl, tv = fill(p, pool, [lambda e, h, j=j: e.transpose(
            h[0:64, 0:64].bitcast(BF16), stg_pb[:, j * D:(j + 1) * D], identb[:, :])],
            rd=("stg_pb", "identb"))
        evac(p, "v", sl, tv, lambda e, h, j=j: e.tensor_copy(
            projLo[0:64, j * 128:(j + 1) * 128], h[0:64, 0:64].bitcast(BF16)),
            **({"wrpost": ("projLo",)} if j == 1 else {}))
    p.op("y", lambda e: e.dma_start(out=projHi[64:128, :], in_=projLo[0:64, :]),
         inc="dC", amt=16, rd=("projLo",), wrpost=("projHi",), waits=[("sV", pmz)])
    ldc(grow[:, :], ap["gamma"].rearrange("(one f) -> one f", one=1), wr=("grow",))
    ldc(brow[:, :], ap["beta"].rearrange("(one f) -> one f", one=1), wr=("brow",))

    p.phase = "P1"
    p.barrier()
    r0.close()
    w1t = mk(r1, "w1t", [128, 16 * DM], BF16)
    w2kb = mk(r1, "w2kb", [128, 16 * HID], BF16)
    stg1 = [mk(r1, f"stg1_{i}", [128, HID], F32) for i in range(2)]
    stg1b = mk(r1, "stg1b", [128, HID], BF16)
    b1b = mk(r1, "b1b", [128, 16], BF16)
    stgb2 = mk(r1, "stgb2", [1, 512], F32)
    stgb12 = mk(r1, "stgb12", [1, 512], F32)

    def ldp(dst, src, wr, rd=()):
        return p.op("y", lambda e, d=dst, s=src: e.dma_start(out=d, in_=s),
                    inc="dP", amt=16, wr=wr, rd=rd)

    p.op("y", lambda e: e.dma_start(out=stg1[0][:, 0:16],
        in_=ap["b1e"].rearrange("(k p) -> p k", p=128),
        allow_slow_non_contiguous=True), inc="dP", amt=16, wr=("stg1_0",))
    p.op("v", lambda e: e.tensor_copy(b1b[:, :], stg1[0][:, 0:16]),
         inc="sV", rd=("stg1_0",), wr=("b1b",))

    for dt_ in range(KT):
        pr1 = dt_ % 2
        p.op("y", lambda e, dt_=dt_, pr1=pr1: e.dma_start(
            out=stg1[pr1][:, :], in_=ap["w1e"][dt_ * 128:(dt_ + 1) * 128, :]),
            inc="dP" if pr1 == 0 else "dP2", amt=16, wr=(f"stg1_{pr1}",))
        p.op("v", lambda e, pr1=pr1: e.tensor_copy(stg1b[:, :], stg1[pr1][:, :]),
             inc="sV", rd=(f"stg1_{pr1}",), wr=("stg1b",))
        for j in range(16):
            sl, tv = fill(p, pool, [lambda e, h, j=j: e.transpose(
                bf16v(h, 0, 128), stg1b[:, j * 128:(j + 1) * 128], identb[:, :])],
                rd=("stg1b", "identb"))
            evac(p, "v", sl, tv, lambda e, h, j=j, dt_=dt_: e.tensor_copy(
                w1t[:, j * DM + dt_ * 128:j * DM + (dt_ + 1) * 128], bf16v(h, 0, 128)),
                wr=(f"w1t{j}_{dt_}",))
    for k in range(16):
        pr1 = k % 2
        p.op("y", lambda e, k=k, pr1=pr1: e.dma_start(
            out=stg1[pr1][:, :], in_=ap["w2e"][k * 128:(k + 1) * 128, :]),
            inc="dP" if pr1 == 0 else "dP2", amt=16, wr=(f"stg1_{pr1}",))
        p.op("v", lambda e, k=k, pr1=pr1: e.tensor_copy(
            w2kb[:, k * HID:(k + 1) * HID], stg1[pr1][:, :]),
            inc="sV", rd=(f"stg1_{pr1}",), wr=(f"w2kb{k}",))
    for k in range(16):
        pr1 = k % 2
        p.op("y", lambda e, k=k, pr1=pr1: e.dma_start(
            out=stg1[pr1][:, 0:DM], in_=ap["w3e"][k * 128:(k + 1) * 128, :]),
            inc="dP" if pr1 == 0 else "dP2", amt=16, wr=(f"stg1_{pr1}",))
        p.op("v", lambda e, pr1=pr1: e.tensor_copy(stg1b[:, 0:DM], stg1[pr1][:, 0:DM]),
             inc="sV", rd=(f"stg1_{pr1}",), wr=("stg1b",))
        p.op("s", lambda e, k=k: e.dma_start(
            out=ccin_w.ap()[DM + 1 + k * 64:DM + 1 + (k + 1) * 64, :]
                .rearrange("a (b c) -> (a b) c", b=2),
            in_=stg1b[:, 0:DM]), inc="dSt", amt=16, rd=("stg1b",))

    w12rd = tuple([f"w1t{jg}_{dt_}" for jg in range(2) for dt_ in range(KT)] +
                  [f"w2kb{k}" for k in range(16)])
    for mt in range(KT):
        for q4 in range(4):
            mms = []
            for k in range(16):
                mms.append(lambda e, h, mt=mt, q4=q4, k=k: e.matmul(
                    h[:, :], w1t[:, k * DM + mt * 128:k * DM + (mt + 1) * 128],
                    w2kb[:, k * HID + q4 * 512:k * HID + (q4 + 1) * 512],
                    start=(k == 0), stop=(k == 15)))
            sl, tv = fill(p, pool, mms, rd=w12rd if (mt == 0 and q4 == 0) else ())
            evac(p, "v", sl, tv, lambda e, h, q4=q4: e.tensor_copy(
                stg1b[:, q4 * 512:(q4 + 1) * 512], h[:, :]),
                **(dict(wrpre=("stg1b",)) if q4 == 0 else
                   (dict(wrpost=("stg1b",)) if q4 == 3 else {})))
        p.op("s", lambda e, mt=mt: e.dma_start(
            out=ccin_w.ap()[mt * 128:(mt + 1) * 128, :], in_=stg1b[:, :]),
            inc="dSt", amt=16, rd=("stg1b",))
    for q4 in range(4):
        ldp(stgb2[:, :], ap["b2e"][q4 * 512:(q4 + 1) * 512]
            .rearrange("(one f) -> one f", one=1), wr=("stgb2",))
        mms = []
        for k in range(16):
            mms.append(lambda e, h, q4=q4, k=k: e.matmul(
                h[0:1, :], b1b[:, k:k + 1],
                w2kb[:, k * HID + q4 * 512:k * HID + (q4 + 1) * 512],
                start=(k == 0), stop=(k == 15)))
        sl, tv = fill(p, pool, mms, rd=("b1b",) if q4 == 0 else ())
        evac(p, "v", sl, tv, lambda e, h, q4=q4: e.tensor_copy(
            stgb12[:, :], h[0:1, :]), wr=("stgb12",))
        p.op("v", lambda e, q4=q4: e.tensor_tensor(
            stg1b[0:1, q4 * 512:(q4 + 1) * 512], stgb12[:, :], stgb2[:, :], ALU.add),
            inc="sV", rd=("stgb12", "stgb2"),
            **(dict(wrpre=("stg1b",)) if q4 == 0 else
               (dict(wrpost=("stg1b",)) if q4 == 3 else {})))
    p.op("s", lambda e: e.dma_start(out=ccin_w.ap()[DM:DM + 1, :], in_=stg1b[0:1, :]),
         inc="dSt", amt=16, rd=("stg1b",))
    w12_stores_done = p.val("dSt")

    p.barrier()
    r1.close()
    state = mk(r2b, "state", [128, ST_W], F32)
    stateb = mk(r2b, "stateb", [128, ST_W], BF16)
    x0g = [mk(r2b, f"x0g{i}", [128, spg * DM], F32) for i in range(2)]
    ln1m = mk(r2b, "ln1m", [128, 2 * spg], F32)
    ln1r = mk(r2b, "ln1r", [128, 2 * spg], F32)
    z1bA = [mk(r2b, f"z1b{i}", [128, DM], BF16) for i in range(2)]
    xn1f = mk(r2b, "xn1f", [128, KT * GRP], BF16)
    qfm = mk(r2b, "qfm", [128, KT * GRP], BF16)
    kfm = mk(r2b, "kfm", [128, DM], BF16)
    vtm = mk(r2b, "vtm", [128, DM], BF16)
    sqfm = mk(r2b, "sqfm", [128, DM], F32)
    ckf = mk(r2b, "ckf", [128, H], F32)
    eqp = mk(r2b, "eqp", [128, H * 2 * 128], BF16)
    ekp = mk(r2b, "ekp", [128, H * 2 * 128], BF16)
    kpt = mk(r2b, "kpt", [128, H * M], BF16)
    amb = [mk(r2b, f"amb{i}", [128, 256], BF16) for i in range(2)]
    rcol = mk(r2b, "rcol", [128, 2], F32)
    atm = mk(r2b, "atm", [128, DM], BF16)
    afm = mk(r2b, "afm", [128, DM], BF16)
    resf = mk(r2b, "resf", [128, DM], F32)
    scr2 = mk(r2b, "scr2", [128, DM], F32)
    stS = mk(r2b, "stS", [128, ST_W // 4], F32)
    lnx = mk(r2b, "lnx", [128, 2], F32)
    gammaB = mk(r2b, "gammaB", [128, DM], F32)
    betaB = mk(r2b, "betaB", [128, DM], F32)
    for srow, dst, key in ((grow, gammaB, "gammaB"), (brow, betaB, "betaB")):
        for half in range(2):
            sl, tv = fill(p, pool, [lambda e, h, half=half, srow=srow: e.matmul(
                h[:, :], onesf[0:1, :], srow[0:1, half * 512:(half + 1) * 512],
                start=True, stop=True)], rd=("onesf", "grow", "brow"))
            evac(p, "v", sl, tv, lambda e, h, d=dst, half=half: e.tensor_copy(
                d[:, half * 512:(half + 1) * 512], h[:, :]),
                **(dict(wrpre=(key,)) if half == 0 else dict(wrpost=(key,))))

    def emit_ln_stats(src_key, src_ap, eps):
        p.op("v", lambda e: e.reduce_sum(stat[:, 40:41], src_ap, axis=AX.X),
             inc="sV", rd=(src_key,), wr=("stat2",))
        p.op("s", lambda e: e.activation(sqscr[:, :], src_ap[:, 0:512], AF.Square,
                                         accum_out=stat[:, 41:42]),
             inc="sS", rd=(src_key,), wr=("stat2", "sqscr"))
        p.op("s", lambda e: e.activation(sqscr2[:, :], src_ap[:, 512:1024], AF.Square,
                                         accum_out=stat[:, 42:43]),
             inc="sS", rd=(src_key,), wr=("stat2", "sqscr2"))
        p.op("v", lambda e: e.tensor_tensor(stat[:, 41:42], stat[:, 41:42],
                                            stat[:, 42:43], ALU.add),
             inc="sV", rd=("stat2",), wr=("stat2",))
        p.op("v", lambda e: e.tensor_scalar(stat[:, 43:44], stat[:, 40:41], 1.0 / DM,
                                            None, ALU.mult),
             inc="sV", rd=("stat2",), wr=("stat2",))
        p.op("v", lambda e: e.tensor_tensor(stat[:, 44:45], stat[:, 43:44],
                                            stat[:, 43:44], ALU.mult),
             inc="sV", rd=("stat2",), wr=("stat2",))
        p.op("v", lambda e: e.tensor_scalar(stat[:, 41:42], stat[:, 41:42], 1.0 / DM,
                                            None, ALU.mult),
             inc="sV", rd=("stat2",), wr=("stat2",))
        p.op("v", lambda e: e.tensor_tensor(stat[:, 45:46], stat[:, 41:42],
                                            stat[:, 44:45], ALU.subtract),
             inc="sV", rd=("stat2",), wr=("stat2",))
        p.op("v", lambda e, eps=eps: e.tensor_scalar(stat[:, 45:46], stat[:, 45:46],
                                                     eps, None, ALU.add),
             inc="sV", rd=("stat2",), wr=("stat2",))
        p.op("v", lambda e: e.reciprocal(stat[:, 46:47], stat[:, 45:46]),
             inc="sV", rd=("stat2",), wr=("stat2",))
        sqv = p.op("s", lambda e: e.activation(stat[:, 48:49], stat[:, 46:47], AF.Sqrt),
                   inc="sS", rd=("stat2",), wr=("stat2s",))
        p.op("v", lambda e: e.tensor_scalar(stat[:, 47:48], stat[:, 43:44], -1.0,
                                            None, ALU.mult),
             inc="sV", rd=("stat2",), wr=("stat2",), waits=[("sS", sqv)])

    def emit_ln_stats_group(srcs, eps, mdst, rdst, gkey):
        n = len(srcs)
        for i, (k, apx) in enumerate(srcs):
            p.op("v", lambda e, i=i, apx=apx: e.reduce_sum(
                stat[:, i:i + 1], apx, axis=AX.X),
                inc="sV", rd=(k,),
                wrpre=("statS",) if i == 0 else (),
                wrpost=("statS",) if i == n - 1 else ())
        for i, (k, apx) in enumerate(srcs):
            for half in range(2):
                dump = sqscr if (2 * i + half) % 2 == 0 else sqscr2
                dk = "sqscr" if (2 * i + half) % 2 == 0 else "sqscr2"
                col = 8 + half * 8 + i
                first = (i == 0 and half == 0)
                last = (i == n - 1 and half == 1)
                p.op("s", lambda e, apx=apx, half=half, col=col, dump=dump:
                     e.activation(dump[:, :], apx[:, half * 512:(half + 1) * 512],
                                  AF.Square, accum_out=stat[:, col:col + 1]),
                     inc="sS", rd=(k,), wr=(dk,),
                     wrpre=("statQ",) if first else (),
                     wrpost=("statQ",) if last else ())
        p.op("v", lambda e, n=n: e.tensor_tensor(
            stat[:, 8:8 + n], stat[:, 8:8 + n], stat[:, 16:16 + n], ALU.add),
            inc="sV", rd=("statQ",), wr=("statQ",))
        p.op("v", lambda e, n=n: e.tensor_scalar(
            stat[:, 24:24 + n], stat[:, 0:n], 1.0 / DM, None, ALU.mult),
            inc="sV", rd=("statS",), wr=("statM",))
        p.op("v", lambda e, n=n: e.tensor_tensor(
            stat[:, 32:32 + n], stat[:, 24:24 + n], stat[:, 24:24 + n], ALU.mult),
            inc="sV", rd=("statM",), wr=("statV",))
        p.op("v", lambda e, n=n: e.tensor_scalar(
            stat[:, 8:8 + n], stat[:, 8:8 + n], 1.0 / DM, None, ALU.mult),
            inc="sV", rd=("statQ",), wr=("statQ",))
        p.op("v", lambda e, n=n: e.tensor_tensor(
            stat[:, 8:8 + n], stat[:, 8:8 + n], stat[:, 32:32 + n], ALU.subtract),
            inc="sV", rd=("statQ", "statV"), wr=("statQ",))
        p.op("v", lambda e, n=n, eps=eps: e.tensor_scalar(
            stat[:, 8:8 + n], stat[:, 8:8 + n], eps, None, ALU.add),
            inc="sV", rd=("statQ",), wr=("statQ",))
        p.op("v", lambda e, n=n: e.reciprocal(stat[:, 64:64 + n], stat[:, 8:8 + n]),
            inc="sV", rd=("statQ",), wr=("statR",))
        sqv = p.op("s", lambda e, n=n, rdst=rdst: e.activation(
            rdst[:, 0:n], stat[:, 64:64 + n], AF.Sqrt),
            inc="sS", rd=("statR",), wrpre=(gkey,))
        p.op("v", lambda e, n=n, mdst=mdst: e.tensor_scalar(
            mdst[:, 0:n], stat[:, 24:24 + n], -1.0, None, ALU.mult),
            inc="sV", rd=("statM",), wrpost=(gkey,), waits=[("sS", sqv)])

    def emit_ln_stats_to(src_key, src_ap, eps, mdst, rdst, gkey):
        p.op("v", lambda e, src_ap=src_ap: e.reduce_sum(stat[:, 40:41], src_ap,
                                                        axis=AX.X),
             inc="sV", rd=(src_key,), wr=("stat2",))
        p.op("s", lambda e, src_ap=src_ap: e.activation(
            sqscr[:, :], src_ap[:, 0:512], AF.Square, accum_out=stat[:, 41:42]),
             inc="sS", rd=(src_key,), wr=("stat2", "sqscr"))
        p.op("s", lambda e, src_ap=src_ap: e.activation(
            sqscr2[:, :], src_ap[:, 512:1024], AF.Square, accum_out=stat[:, 42:43]),
             inc="sS", rd=(src_key,), wr=("stat2", "sqscr2"))
        p.op("v", lambda e: e.tensor_tensor(stat[:, 41:42], stat[:, 41:42],
                                            stat[:, 42:43], ALU.add),
             inc="sV", rd=("stat2",), wr=("stat2",))
        p.op("v", lambda e: e.tensor_scalar(stat[:, 43:44], stat[:, 40:41], 1.0 / DM,
                                            None, ALU.mult),
             inc="sV", rd=("stat2",), wr=("stat2",))
        p.op("v", lambda e: e.tensor_tensor(stat[:, 44:45], stat[:, 43:44],
                                            stat[:, 43:44], ALU.mult),
             inc="sV", rd=("stat2",), wr=("stat2",))
        p.op("v", lambda e: e.tensor_scalar(stat[:, 41:42], stat[:, 41:42], 1.0 / DM,
                                            None, ALU.mult),
             inc="sV", rd=("stat2",), wr=("stat2",))
        p.op("v", lambda e: e.tensor_tensor(stat[:, 45:46], stat[:, 41:42],
                                            stat[:, 44:45], ALU.subtract),
             inc="sV", rd=("stat2",), wr=("stat2",))
        p.op("v", lambda e, eps=eps: e.tensor_scalar(stat[:, 45:46], stat[:, 45:46],
                                                     eps, None, ALU.add),
             inc="sV", rd=("stat2",), wr=("stat2",))
        p.op("v", lambda e: e.reciprocal(stat[:, 46:47], stat[:, 45:46]),
             inc="sV", rd=("stat2",), wr=("stat2",))
        sqv = p.op("s", lambda e, rdst=rdst: e.activation(rdst, stat[:, 46:47],
                                                          AF.Sqrt),
                   inc="sS", rd=("stat2",), wrpre=(gkey,))
        p.op("v", lambda e, mdst=mdst: e.tensor_scalar(mdst, stat[:, 43:44], -1.0,
                                                       None, ALU.mult),
             inc="sV", rd=("stat2",), wrpost=(gkey,), waits=[("sS", sqv)])

    def emit_xn1_transpose(s, first_sub=True, last_sub=True):
        zb, zk = z1bA[s % 2], f"z1b{s % 2}"
        mms = [lambda e, h, k=k, zb=zb: e.transpose(
            h[:, k * 64:(k + 1) * 64].bitcast(BF16),
            zb[:, k * 128:(k + 1) * 128], identb[:, :]) for k in range(KT)]
        sl, tv = fill(p, pool, mms, rd=(zk, "identb"))
        dst = xn1f[:, :].rearrange("p (k g) -> p k g", k=KT)[:, :, s * 128:(s + 1) * 128]
        evac(p, "v", sl, tv, lambda e, h, dst=dst: e.tensor_copy(
            dst, h[:, :].bitcast(BF16).rearrange("p (k g) -> p k g", k=KT)),
            wrpre=("xn1f",) if first_sub else (),
            wrpost=("xn1f",) if last_sub else ())

    def emit_kv_sub(s):
        for half in range(2):
            mms = []
            for k in range(KT):
                mms.append(lambda e, h, k=k, half=half, s=s: e.matmul(
                    h[:, :], xn1f[:, k * GRP + s * 128:k * GRP + (s + 1) * 128],
                    wvb[:, k * DM + half * 512:k * DM + (half + 1) * 512],
                    start=(k == 0), stop=False))
            mms.append(lambda e, h, half=half: e.matmul(
                h[:, :], onesb[0:1, :], vbr[0:1, half * 512:(half + 1) * 512],
                start=False, stop=True))
            sl, tv = fill(p, pool, mms, rd=("xn1f", "wvb", "onesb", "vbr"))
            evac(p, "v", sl, tv, lambda e, h, half=half: e.tensor_copy(
                vtm[:, half * 512:(half + 1) * 512], h[:, :]),
                wr=(f"vtmc{2 * half}", f"vtmc{2 * half + 1}"))
        for o in range(KT):
            mms = []
            for k in range(KT):
                mms.append(lambda e, h, o=o, k=k, s=s: e.matmul(
                    h[:, 0:128], wkb[:, k * DM + o * 128:k * DM + (o + 1) * 128],
                    xn1f[:, k * GRP + s * 128:k * GRP + (s + 1) * 128],
                    start=(k == 0), stop=(k == KT - 1)))
            sl, tv = fill(p, pool, mms, rd=("xn1f", "wkb"))
            kw = dict(wrpre=("kfm",)) if o == 0 else {}
            if o == KT - 1:
                kw = dict(wr=("kfm",))
            evac(p, "v", sl, tv, lambda e, h, o=o: e.tensor_scalar(
                kfm[:, o * 128:(o + 1) * 128], h[:, 0:128], kbt[:, o:o + 1],
                None, ALU.add), rd=("kbt",), **kw)
        for o in range(KT):
            kw = dict(wrpre=("sqfm",)) if o == 0 else {}
            if o == KT - 1:
                kw = dict(wrpost=("sqfm",))
            p.op("s", lambda e, o=o: e.activation(
                sqfm[:, o * 128:(o + 1) * 128], kfm[:, o * 128:(o + 1) * 128],
                AF.Square), inc="sS", rd=("kfm",), **kw)
        mms = []
        for h_ in range(H):
            mms.append(lambda e, h, h_=h_: e.matmul(
                h[:, h_:h_ + 1], sqfm[:, (h_ // 2) * 128:(h_ // 2 + 1) * 128],
                oneslh[:, (h_ % 2):(h_ % 2) + 1],
                start=(h_ == 0), stop=(h_ == H - 1), skip_group_check=True))
        sl, tv = fill(p, pool, mms, rd=("sqfm", "oneslh"))
        evac(p, "v", sl, tv, lambda e, h: e.tensor_scalar(
            ckf[:, :], h[:, 0:H], -1.0 / 128.0, -float(np.log(16.0)),
            ALU.mult, ALU.add), wr=("ckf",))
        for t4 in range(KT):
            mms = [lambda e, hs, t4=t4, pi=pi: e.matmul(
                hs[pi][:, 0:M], kfm[:, t4 * 128:(t4 + 1) * 128],
                (projLo if pi == 0 else projHi)[:, :],
                start=True, stop=True) for pi in range(2)]
            sls, tv = fill_multi(p, pool, 2, mms, rd=("kfm", "projLo", "projHi"))
            for pi in range(2):
                h_ = t4 * 2 + pi
                kw = dict(wrpre=(f"kptc{h_ // 4}",)) if h_ % 4 == 0 else {}
                if h_ % 4 == 3:
                    kw = dict(wrpost=(f"kptc{h_ // 4}",))
                evac(p, "s", sls[pi], tv, lambda e, h, h_=h_: e.activation(
                    kpt[:, h_ * M:(h_ + 1) * M], h[:, 0:M], AF.Exp,
                    bias=ckf[:, h_:h_ + 1]), rd=("ckf",), **kw)

    def emit_q_and_maps(s):
        for h_ in range(H):
            pv = projLo if h_ % 2 == 0 else projHi
            pk = "projLo" if h_ % 2 == 0 else "projHi"
            mms = []
            for j in range(2):
                mms.append(lambda e, h, h_=h_, j=j, s=s, pv=pv: e.matmul(
                    h[:, j * 128:(j + 1) * 128], pv[:, j * 128:(j + 1) * 128],
                    qfm[:, (h_ // 2) * GRP + s * 128:(h_ // 2) * GRP + (s + 1) * 128],
                    start=True, stop=True))
            sl, tv = fill(p, pool, mms, rd=("qfm", pk))
            kw = dict(wrpre=(f"eqpc{h_ // 4}",)) if h_ % 4 == 0 else {}
            if h_ % 4 == 3:
                kw = dict(wrpost=(f"eqpc{h_ // 4}",))
            evac(p, "s", sl, tv, lambda e, h, h_=h_: e.activation(
                eqp[:, h_ * 256:(h_ + 1) * 256], h[:, 0:256], AF.Exp), **kw)

    def emit_state_update(h_, phase_c):
        mms = []
        for j in range(2):
            mms.append(lambda e, h, h_=h_, j=j: e.matmul(
                h[:, j * 64:(j + 1) * 64], kpt[:, h_ * M + j * 128:h_ * M + (j + 1) * 128],
                vtm[:, h_ * D:(h_ + 1) * D], start=True, stop=True))
        for j in range(2):
            mms.append(lambda e, h, h_=h_, j=j: e.matmul(
                h[:, 128 + j:129 + j], kpt[:, h_ * M + j * 128:h_ * M + (j + 1) * 128],
                onescb[:, :], start=True, stop=True))
        sl, tv = fill(p, pool, mms,
                      rd=(f"kptc{h_ // 4}", f"vtmc{h_ // 4}", "onescb"))
        c0 = h_ * (2 * D + 2)
        evac(p, "v", sl, tv, lambda e, h, c0=c0: e.tensor_tensor(
            state[:, c0:c0 + 130], state[:, c0:c0 + 130], h[:, 0:130], ALU.add),
            rd=(f"st{h_}",), wr=(f"st{h_}",))
        if phase_c:
            p.op("s", lambda e, c0=c0: e.activation(
                stateb[:, c0:c0 + 130], state[:, c0:c0 + 130], AF.Copy),
                inc="sS", rd=(f"st{h_}",), wr=(f"stb{h_}",))

    p.phase = "A"
    p.op("v", lambda e: e.memset(state[:, :], 0.0), inc="sV",
         wr=tuple(f"st{h_}" for h_ in range(H)))

    def x0_load(g, par):
        for s in range(spg):
            kw = dict(wrpre=(f"x0g{par}",)) if s == 0 else {}
            if s == spg - 1:
                kw["wrpost"] = (f"x0g{par}",)
            p.op("y", lambda e, g=g, s=s, par=par: e.dma_start(
                out=x0g[par][:, s * DM:(s + 1) * DM],
                in_=x0d[g * GRP + s * 128:g * GRP + (s + 1) * 128, :]),
                inc="dX", amt=16, **kw)
        return p.val("dX")

    def ln1_group(g):
        par = g % 2
        emit_ln_stats_group(
            [(f"x0g{par}", x0g[par][:, s * DM:(s + 1) * DM]) for s in range(spg)],
            LNE1, ln1m[:, par * spg:], ln1r[:, par * spg:], f"ln1g{par}")

    xv = {0: x0_load(0, 0)}
    for g in range(NG):
        par = g % 2
        if g == 0:
            ln1_group(0)
        if g + 1 < NG:
            xv[g + 1] = x0_load(g + 1, (g + 1) % 2)
            ln1_group(g + 1)
        for s in range(spg):
            stm = par * spg + s
            p.op("v", lambda e, s=s, par=par, stm=stm: e.tensor_scalar(
                z1bA[s % 2][:, :], x0g[par][:, s * DM:(s + 1) * DM],
                ln1m[:, stm:stm + 1],
                ln1r[:, stm:stm + 1], ALU.add, ALU.mult),
                inc="sV", rd=(f"x0g{par}", f"ln1g{par}"), wr=(f"z1b{s % 2}",))
            emit_xn1_transpose(s)
            p.op("y", lambda e, g=g, s=s, stm=stm: e.dma_start(
                out=lncache.ap()[g * GRP + s * 128:g * GRP + (s + 1) * 128, 0:1],
                in_=ln1m[:, stm:stm + 1], allow_slow_non_contiguous=True),
                inc="dK", amt=16, rd=(f"ln1g{par}",))
            p.op("y", lambda e, g=g, s=s, stm=stm: e.dma_start(
                out=lncache.ap()[g * GRP + s * 128:g * GRP + (s + 1) * 128, 1:2],
                in_=ln1r[:, stm:stm + 1], allow_slow_non_contiguous=True),
                inc="dK", amt=16, rd=(f"ln1g{par}",))
        for o in range(KT):
            mms = []
            for k in range(KT):
                mms.append(lambda e, h, o=o, k=k: e.matmul(
                    h[:, 0:GRP], wqb[:, k * DM + o * 128:k * DM + (o + 1) * 128],
                    xn1f[:, k * GRP:(k + 1) * GRP],
                    start=(k == 0), stop=(k == KT - 1)))
            sl, tv = fill(p, pool, mms, rd=("xn1f", "wqb"))
            kw = dict(wrpre=("qfm",)) if o == 0 else {}
            if o == KT - 1:
                kw = dict(wr=("qfm",))
            evac(p, "v", sl, tv, lambda e, h, o=o: e.tensor_scalar(
                qfm[:, o * GRP:(o + 1) * GRP], h[:, 0:GRP], qbt[:, o:o + 1],
                None, ALU.add), rd=("qbt",), **kw)
        for s in range(spg):
            row0 = g * GRP + s * 128
            emit_kv_sub(s)
            emit_q_and_maps(s)
            p.op("y", lambda e, row0=row0: e.dma_start(
                out=kcache.ap()[row0:row0 + 128, KC_EQP:KC_EQP + 4096],
                in_=eqp[:, :]), inc="dK", amt=16,
                rd=tuple(f"eqpc{c}" for c in range(4)))
            p.op("y", lambda e, row0=row0: e.dma_start(
                out=kcache.ap()[row0:row0 + 128, KC_KPT:KC_KPT + 4096],
                in_=kpt[:, :]), inc="dK", amt=16,
                rd=tuple(f"kptc{c}" for c in range(4)))
            p.op("y", lambda e, row0=row0: e.dma_start(
                out=kcache.ap()[row0:row0 + 128, KC_VTM:KC_VTM + 1024],
                in_=vtm[:, :]), inc="dK", amt=16,
                rd=tuple(f"vtmc{c}" for c in range(4)))
            for h_ in range(H):
                emit_state_update(h_, phase_c=False)

    p.phase = "CC"
    p.op("g", lambda e: e.collective_compute(
        "AllGather", ALU.bypass, replica_groups=[list(range(NCORES))],
        ins=[ccin_w.ap().opt()], outs=[ccout_w.ap().opt()]),
        inc="gC", waits=[("dSt", w12_stores_done)])
    p.op("s", lambda e: e.dma_start(out=ccin_s.ap(), in_=state[:, :]),
         inc="dSt", amt=16, rd=tuple(f"st{h_}" for h_ in range(H)))
    st_staged = p.val("dSt")
    p.op("g", lambda e: e.collective_compute(
        "AllGather", ALU.bypass, replica_groups=[list(range(NCORES))],
        ins=[ccin_s.ap().opt()], outs=[ccout_s.ap().opt()]),
        inc="gC", waits=[("dSt", st_staged)])
    HW2 = ST_W // 4
    stg_b12 = mk(r2b, "stg_b12", [128, 16], BF16)

    def emit_state_init():
        for half in range(4):
            for r in range(NCORES):
                hkeys = tuple(f"st{h_}" for h_ in range(half * 4, half * 4 + 4))
                p.op("y", lambda e, r=r, half=half: e.dma_start(
                    out=stS[:, :],
                    in_=ccout_s.ap()[r * 128:(r + 1) * 128,
                                     half * HW2:(half + 1) * HW2]),
                    inc="dG", amt=16, wr=("stS",), waits=[("gC", 2)])
                if r == 0:
                    p.op("v", lambda e, half=half: e.tensor_scalar(
                        state[:, half * HW2:(half + 1) * HW2], stS[:, :], pmv[:, 0:1],
                        None, ALU.mult),
                        inc="sV", rd=("stS", "pmv"), wr=hkeys)
                else:
                    p.op("v", lambda e, r=r, half=half: e.scalar_tensor_tensor(
                        state[:, half * HW2:(half + 1) * HW2], stS[:, :],
                        pmv[:, r:r + 1],
                        state[:, half * HW2:(half + 1) * HW2], ALU.mult, ALU.add),
                        inc="sV", rd=("stS", "pmv") + hkeys, wr=hkeys)
            p.op("v", lambda e, half=half: e.tensor_copy(
                stateb[:, half * HW2:(half + 1) * HW2],
                state[:, half * HW2:(half + 1) * HW2]),
                inc="sV",
                rd=tuple(f"st{h_}" for h_ in range(half * 4, half * 4 + 4)),
                wr=tuple(f"stb{h_}" for h_ in range(half * 4, half * 4 + 4)))

    def emit_b12_load():
        for e_ in range(NE):
            p.op("y", lambda e, e_=e_: e.dma_start(
                out=stg_b12[:, :],
                in_=ccout_w.ap()[e_ * WROWS + DM, :].rearrange("(k p) -> p k", p=128),
                allow_slow_non_contiguous=True),
                inc="dG", amt=16, wr=("stg_b12",), waits=[("gC", 1)])
            p.op("v", lambda e, e_=e_: e.tensor_copy(b12f[:, e_ * 16:(e_ + 1) * 16],
                                                     stg_b12[:, :]),
                 inc="sV", rd=("stg_b12",), wr=("b12f",) if e_ == 0 else ())

    p.phase = "C"
    x1_store_vals = []
    xv = {0: x0_load(0, 0)}
    for g in range(NG):
        par = g % 2
        if g + 1 < NG:
            xv[g + 1] = x0_load(g + 1, (g + 1) % 2)
        if g == min(1, NG - 1):
            emit_b12_load()
        if g == 0:
            emit_state_init()
        for s in range(spg):
            row0 = g * GRP + s * 128
            nld = 0
            lnx_w = []
            for ci in range(4):
                for name, base, width, key in (
                        ("eqp", KC_EQP, 1024, f"eqpc{ci}"),
                        ("kpt", KC_KPT, 1024, f"kptc{ci}"),
                        ("vtm", KC_VTM, 256, f"vtmc{ci}")):
                    dst = {"eqp": eqp, "kpt": kpt, "vtm": vtm}[name]
                    cls = "dL" if nld % 2 == 0 else "dL2"
                    p.op("y", lambda e, dst=dst, ci=ci, base=base, width=width,
                         row0=row0: e.dma_start(
                             out=dst[:, ci * width:(ci + 1) * width],
                             in_=kcache.ap()[row0:row0 + 128,
                                             base + ci * width:base + (ci + 1) * width]),
                         inc=cls, amt=16, wrpre=(key,), gate=(nld < 2))
                    lnx_w.append((cls, key))
                    nld += 1
            p.op("y", lambda e, row0=row0: e.dma_start(
                out=lnx[:, :], in_=lncache.ap()[row0:row0 + 128, :]),
                inc="dL", amt=16, wrpre=("lnx",), gate=False)
            vL, vL2 = p.val("dL"), p.val("dL2")
            for cls, key in lnx_w:
                p.hz(key).note_wr(cls, vL if cls == "dL" else vL2)
            p.hz("lnx").note_wr("dL", vL)
            for hp in range(H // 2):
                mms = [lambda e, h, hp=hp, t_=t_: e.transpose(
                    h[:, t_ * 64:(t_ + 1) * 64].bitcast(BF16),
                    kpt[:, (hp * 4 + t_) * 128:(hp * 4 + t_ + 1) * 128],
                    identb[:, :]) for t_ in range(4)]
                sl, tv = fill(p, pool, mms, rd=(f"kptc{hp // 2}", "identb"))
                kw = dict(wrpre=(f"ekpc{hp // 2}",)) if hp % 2 == 0 else \
                    dict(wrpost=(f"ekpc{hp // 2}",))
                evac(p, "v", sl, tv, lambda e, h, hp=hp: e.tensor_copy(
                    ekp[:, hp * 512:(hp + 1) * 512], h[:, 0:256].bitcast(BF16)), **kw)
            def at_am(hp):
                mms = []
                for hh in range(2):
                    for j in range(2):
                        h_ = hp * 2 + hh
                        mms.append(lambda e, h, h_=h_, hh=hh, j=j: e.matmul(
                            h[:, hh * 128:(hh + 1) * 128],
                            ekp[:, (h_ * 2 + j) * 128:(h_ * 2 + j + 1) * 128],
                            eqp[:, (h_ * 2 + j) * 128:(h_ * 2 + j + 1) * 128],
                            start=(hh == 0 and j == 0), stop=(hh == 1 and j == 1),
                            skip_group_check=True))
                sl_a, tva = fill(p, pool, mms,
                                 rd=(f"ekpc{hp // 2}", f"eqpc{hp // 2}"))
                slot = hp % 2
                mt3 = masktri[:, :].rearrange("p (one f) -> p one f", one=1) \
                    .broadcast_to([128, 2, 128])
                evac(p, "v", sl_a, tva, lambda e, h, slot=slot, mt3=mt3: e.tensor_tensor(
                    amb[slot][:, :].rearrange("p (g f) -> p g f", g=2),
                    h[:, 0:256].rearrange("p (g f) -> p g f", g=2), mt3, ALU.mult),
                    rd=("masktri",), wr=(f"amb{slot}",))

            at_am(0)
            for h_ in range(H):
                c0 = h_ * (2 * D + 2)
                hp = h_ // 2
                hh = h_ % 2
                slot = hp % 2
                if hh == 0 and hp + 1 < H // 2:
                    at_am(hp + 1)
                mms = []
                for j in range(2):
                    mms.append(lambda e, h, h_=h_, j=j, c0=c0: e.matmul(
                        h[:, 0:64], eqp[:, (h_ * 2 + j) * 128:(h_ * 2 + j + 1) * 128],
                        stateb[:, c0 + j * 64:c0 + (j + 1) * 64],
                        start=(j == 0), stop=False))
                mms.append(lambda e, h, h_=h_, slot=slot, hh=hh: e.matmul(
                    h[:, 0:64], amb[slot][:, hh * 128:(hh + 1) * 128],
                    vtm[:, h_ * D:(h_ + 1) * D],
                    start=False, stop=True))
                for j in range(2):
                    mms.append(lambda e, h, h_=h_, j=j, c0=c0: e.matmul(
                        h[:, 64:65], eqp[:, (h_ * 2 + j) * 128:(h_ * 2 + j + 1) * 128],
                        stateb[:, c0 + 128 + j:c0 + 129 + j],
                        start=(j == 0), stop=False))
                mms.append(lambda e, h, h_=h_, slot=slot, hh=hh: e.matmul(
                    h[:, 64:65], amb[slot][:, hh * 128:(hh + 1) * 128], onescb[:, :],
                    start=False, stop=True))
                sln, tvn = fill(p, pool, mms,
                                rd=(f"stb{h_}", f"amb{slot}", f"vtmc{h_ // 4}",
                                    "onescb", f"eqpc{h_ // 4}"))
                rslot = h_ % 2
                p.op("v", lambda e, h=sln["h"], rslot=rslot: e.reciprocal(
                    rcol[:, rslot:rslot + 1], h[:, 64:65]),
                    inc="sV", wr=(f"rcol{rslot}",), waits=[("sT", tvn)])
                kw = dict(wrpre=("atm",)) if h_ == 0 else {}
                if h_ == H - 1:
                    kw = dict(wr=("atm",))
                evac(p, "s", sln, tvn, lambda e, h, h_=h_, rslot=rslot: e.activation(
                    atm[:, h_ * D:(h_ + 1) * D], h[:, 0:64], AF.Copy,
                    scale=rcol[:, rslot:rslot + 1]),
                    rd=(f"rcol{rslot}",), **kw)
                emit_state_update(h_, phase_c=True)
            mms = [lambda e, h, k=k: e.transpose(
                h[:, k * 64:(k + 1) * 64].bitcast(BF16),
                atm[:, k * 128:(k + 1) * 128], identb[:, :]) for k in range(KT)]
            sl, tv = fill(p, pool, mms, rd=("atm", "identb"))
            evac(p, "v", sl, tv, lambda e, h: e.tensor_copy(
                afm[:, :], h[:, :].bitcast(BF16)), wr=("afm",))
            for half in range(2):
                mms = []
                for k in range(KT):
                    mms.append(lambda e, h, k=k, half=half: e.matmul(
                        h[:, :], afm[:, k * 128:(k + 1) * 128],
                        wob[:, k * DM + half * 512:k * DM + (half + 1) * 512],
                        start=(k == 0), stop=False))
                mms.append(lambda e, h, half=half: e.matmul(
                    h[:, :], onesb[0:1, :], obr[0:1, half * 512:(half + 1) * 512],
                    start=False, stop=True))
                sl, tv = fill(p, pool, mms, rd=("afm", "wob", "onesb", "obr"))
                kwa = dict(wrpre=("scr2",)) if half == 0 else dict(wrpost=("scr2",))
                zv = p.op("v", lambda e, half=half, s=s, par=par: e.tensor_scalar(
                    scr2[:, half * 512:(half + 1) * 512],
                    x0g[par][:, s * DM + half * 512:s * DM + (half + 1) * 512],
                    lnx[:, 0:1], lnx[:, 1:2], ALU.add, ALU.mult),
                    inc="sV", rd=(f"x0g{par}", "lnx"), **kwa)
                kwc = dict(wrpre=("resf",)) if half == 0 else dict(wrpost=("resf",))
                evac(p, "v", sl, tv, lambda e, h, half=half: e.tensor_tensor(
                    resf[:, half * 512:(half + 1) * 512],
                    scr2[:, half * 512:(half + 1) * 512], h[:, :], ALU.add),
                    waits=[("sV", zv)], **kwc)
            emit_ln_stats("resf", resf[:, :], LNE2)
            p.op("v", lambda e: e.tensor_scalar(scr2[:, :], resf[:, :],
                                                stat[:, 47:48], stat[:, 48:49],
                                                ALU.add, ALU.mult),
                 inc="sV", rd=("resf", "stat2", "stat2s"), wr=("scr2",))
            p.op("v", lambda e: e.tensor_tensor(scr2[:, :], scr2[:, :], gammaB[:, :],
                                                ALU.mult),
                 inc="sV", rd=("scr2", "gammaB"), wr=("scr2",))
            p.op("v", lambda e: e.tensor_tensor(scr2[:, :], scr2[:, :], betaB[:, :],
                                                ALU.add),
                 inc="sV", rd=("scr2", "betaB"), wr=("scr2",))
            p.op("v", lambda e, s=s, par=par: e.tensor_tensor(
                resf[:, :], scr2[:, :], x0g[par][:, s * DM:(s + 1) * DM], ALU.add),
                inc="sV", rd=("scr2", f"x0g{par}"), wr=("resf",))
            x1v = p.op("s", lambda e, g=g, s=s: e.dma_start(
                out=x1scr.ap()[g * GRP + s * 128:g * GRP + (s + 1) * 128, :],
                in_=resf[:, :]), inc="dSt", amt=16, rd=("resf",))
            x1_store_vals.append(x1v)

    p.phase = "D"
    p.barrier()
    r2b.close()
    r2a.close()
    w12b = [mk(r3, f"w12b{i}", [128, KT * HID], BF16) for i in range(2)]
    w3b = [mk(r3, "w3b0", [128, 16 * DM], BF16)] * 2
    xn3f = mk(r3, "xn3f", [128, KT * MTOK], BF16)
    accf = mk(r3, "accf", [128, MSUB * DM], F32)
    a_fm = mk(r3, "a_fm", [128, 16 * MTOK], BF16)
    cwf = mk(r3, "cwf", [128, MSUB * NE], F32)
    cwfb = mk(r3, "cwfb", [128, MSUB * NE], BF16)
    cwfm = mk(r3, "cwfm", [NE, MSUB * 128], BF16)
    logitsb = mk(r3, "logitsb", [128, MSUB * NE], F32)
    usel = mk(r3, "usel", [128, MSUB * NE], F32)
    uexp = mk(r3, "uexp", [128, MSUB * NE], F32)
    l3m = mk(r3, "l3m", [128, 8], F32)
    l3r = mk(r3, "l3r", [128, 8], F32)
    z3bA = [mk(r3, f"z3b{i}", [128, DM], BF16) for i in range(2)]

    def wload(e_, buf):
        for k in range(KT):
            kw = dict(wrpre=(f"w12b{buf}",)) if k == 0 else {}
            if k == KT - 1:
                kw["wrpost"] = (f"w12b{buf}",)
            p.op("y", lambda e, e_=e_, k=k, buf=buf: e.dma_start(
                out=w12b[buf][:, k * HID:(k + 1) * HID],
                in_=ccout_w.ap()[e_ * WROWS + k * 128:e_ * WROWS + (k + 1) * 128, :]),
                inc="dW", amt=16,
                waits=[("gC", 1)] if k == 0 else [], **kw)
        for k in range(16):
            r0 = e_ * WROWS + DM + 1 + k * 64
            kw = dict(wrpre=("w3b",)) if k == 0 else {}
            if k == 15:
                kw["wrpost"] = ("w3b",)
            p.op("y", lambda e, r0=r0, k=k, buf=buf: e.dma_start(
                out=w3b[buf][:, k * DM:(k + 1) * DM],
                in_=ccout_w.ap()[r0:r0 + 64, :].rearrange("a (b c) -> (a b) c", b=2)),
                inc="dW", amt=16, **kw)
        return p.val("dW")

    wload(0, 0)
    def x1_load(mg, s):
        p.op("g", lambda e, mg=mg, s=s: e.dma_start(
            out=accf[:, s * DM:(s + 1) * DM],
            in_=x1scr.ap()[mg * MTOK + s * 128:mg * MTOK + (s + 1) * 128, :]),
            inc="dX1", amt=16, wrpre=(f"accf{s}",),
            waits=[("dSt", x1_store_vals[mg * MSUB + s])])
        p.hz(f"accf{s}").note_wr("dX1", p.val("dX1"))
        emit_ln_stats_to(f"accf{s}", accf[:, s * DM:(s + 1) * DM], LNE1,
                         l3m[:, s:s + 1], l3r[:, s:s + 1], f"l3g{s}")

    for mg in range(MG):
        if mg == 0:
            for s in range(MSUB):
                x1_load(0, s)
        for s in range(MSUB):
            c0 = s * DM
            zb3, zk3 = z3bA[s % 2], f"z3b{s % 2}"
            p.op("v", lambda e, c0=c0, s=s, zb3=zb3: e.tensor_scalar(
                zb3[:, :], accf[:, c0:c0 + DM], l3m[:, s:s + 1], l3r[:, s:s + 1],
                ALU.add, ALU.mult),
                inc="sV", rd=(f"accf{s}", f"l3g{s}"), wr=(zk3,))
            mms = [lambda e, h, k=k, zb3=zb3: e.transpose(
                h[:, k * 64:(k + 1) * 64].bitcast(BF16),
                zb3[:, k * 128:(k + 1) * 128], identb[:, :]) for k in range(KT)]
            sl, tv = fill(p, pool, mms, rd=(zk3, "identb"))
            dst3 = xn3f[:, :].rearrange("p (k g) -> p k g", k=KT)[:, :, s * 128:(s + 1) * 128]
            evac(p, "v", sl, tv, lambda e, h, dst3=dst3: e.tensor_copy(
                dst3, h[:, :].bitcast(BF16).rearrange("p (k g) -> p k g", k=KT)),
                wr=(f"xn3f{s}",))
            mms = []
            for k in range(KT):
                mms.append(lambda e, h, k=k, s=s: e.matmul(
                    h[:, 0:NE], xn3f[:, k * MTOK + s * 128:k * MTOK + (s + 1) * 128],
                    wrb[:, k * NE:(k + 1) * NE], start=(k == 0), stop=False))
            mms.append(lambda e, h: e.matmul(
                h[:, 0:NE], onesb[0:1, :], brb[0:1, :], start=False, stop=True))
            sl, tv = fill(p, pool, mms, rd=(f"xn3f{s}", "wrb", "brb", "onesb"))
            evac(p, "v", sl, tv, lambda e, h, s=s: e.tensor_copy(
                logitsb[:, s * NE:(s + 1) * NE], h[:, 0:NE]),
                wrpre=("lgt",) if s == 0 else (),
                wrpost=("lgt",) if s == MSUB - 1 else ())
        nsub = MSUB
        u3 = lambda t: t[:, :].rearrange("p (s e) -> p s e", s=nsub)
        g3 = lambda c: stat[:, c:c + nsub].rearrange(
            "p (s one) -> p s one", one=1)
        g3b = lambda c: stat[:, c:c + nsub].rearrange(
            "p (s one) -> p s one", one=1).broadcast_to([128, nsub, NE])
        p.op("s", lambda e: e.activation(uexp[:, :], logitsb[:, :], AF.Exp),
             inc="sS", rd=("lgt",), wr=("uexp",))
        p.op("v", lambda e: e.tensor_reduce(g3(48), u3(uexp), AX.X, ALU.max),
             inc="sV", rd=("uexp",), wr=("rtA",))
        p.op("v", lambda e: e.tensor_tensor(u3(usel), u3(uexp), g3b(48), ALU.is_ge),
             inc="sV", rd=("uexp", "rtA"), wr=("usel",))
        p.op("v", lambda e: e.tensor_scalar(usel[:, :], usel[:, :], -1.0, 1.0,
                                            ALU.mult, ALU.add),
             inc="sV", rd=("usel",), wr=("usel",))
        p.op("v", lambda e: e.tensor_tensor(usel[:, :], uexp[:, :], usel[:, :],
                                            ALU.mult),
             inc="sV", rd=("usel", "uexp"), wr=("usel",))
        p.op("v", lambda e: e.tensor_reduce(g3(48), u3(usel), AX.X, ALU.max),
             inc="sV", rd=("usel",), wr=("rtA",))
        p.op("v", lambda e: e.tensor_tensor(u3(usel), u3(uexp), g3b(48), ALU.is_ge),
             inc="sV", rd=("uexp", "rtA"), wr=("usel",))
        p.op("v", lambda e: e.tensor_tensor(usel[:, :], uexp[:, :], usel[:, :],
                                            ALU.mult),
             inc="sV", rd=("usel", "uexp"), wr=("usel",))
        p.op("v", lambda e: e.tensor_reduce(g3(48), u3(usel), AX.X, ALU.add),
             inc="sV", rd=("usel",), wr=("rtA",))
        p.op("v", lambda e: e.reciprocal(stat[:, 56:56 + nsub], stat[:, 48:48 + nsub]),
             inc="sV", rd=("rtA",), wr=("rtB",))
        p.op("v", lambda e: e.tensor_tensor(u3(cwf), u3(usel), g3b(56), ALU.mult),
             inc="sV", rd=("usel", "rtB"), wr=("cwfall",))
        p.op("v", lambda e: e.tensor_copy(cwfb[:, :], cwf[:, :]),
             inc="sV", rd=("cwfall",), wr=("cwfb",))
        for s in range(MSUB):
            c0 = s * DM
            sl, tv = fill(p, pool, [lambda e, h, s=s: e.transpose(
                h[0:NE, 0:64].bitcast(BF16), cwfb[:, s * NE:(s + 1) * NE],
                identb[:, :])],
                rd=("cwfb", "identb"))
            evac(p, "v", sl, tv, lambda e, h, s=s: e.tensor_copy(
                cwfm[:, s * 128:(s + 1) * 128], h[0:NE, 0:64].bitcast(BF16)),
                wr=(f"cwfm{s}",))
            for half in range(2):
                sl, tv = fill(p, pool, [lambda e, h, s=s, half=half: e.matmul(
                    h[:, :], cwfm[0:NE, s * 128:(s + 1) * 128],
                    b3b[0:NE, half * 512:(half + 1) * 512], start=True, stop=True)],
                    rd=(f"cwfm{s}", "b3b"))
                evac(p, "v", sl, tv, lambda e, h, c0=c0, half=half: e.tensor_tensor(
                    accf[:, c0 + half * 512:c0 + (half + 1) * 512],
                    accf[:, c0 + half * 512:c0 + (half + 1) * 512], h[:, :], ALU.add),
                    rd=(f"accf{s}",), wr=(f"accf{s}",))
        for e_ in range(NE):
            gi = mg * NE + e_
            buf = gi % 2
            nhalf = max(1, MTOK // 512)
            wfrac = min(512, MTOK)
            for o in range(16):
                for hh in range(nhalf):
                    mms = []
                    for k in range(KT):
                        mms.append(lambda e, h, o=o, hh=hh, k=k, buf=buf: e.matmul(
                            h[:, 0:wfrac],
                            w12b[buf][:, k * HID + o * 128:k * HID + (o + 1) * 128],
                            xn3f[:, k * MTOK + hh * 512:k * MTOK + hh * 512 + wfrac],
                            start=(k == 0), stop=(k == KT - 1)))
                    sl, tv = fill(p, pool, mms, rd=(f"w12b{buf}",) + tuple(
                        f"xn3f{s2}" for s2 in range(MSUB)))
                    kw = {}
                    if o == 0 and hh == 0:
                        kw["wrpre"] = ("a_fm",)
                    if o == 15 and hh == nhalf - 1:
                        kw["wrpost"] = ("a_fm",)
                    evac(p, "s", sl, tv, lambda e, h, o=o, hh=hh, e_=e_: e.activation(
                        a_fm[:, o * MTOK + hh * 512:o * MTOK + hh * 512 + wfrac],
                        h[:, 0:wfrac], AF.Silu,
                        bias=b12f[:, e_ * 16 + o:e_ * 16 + o + 1]),
                        rd=("b12f",), **kw)
            for s in range(MSUB):
                for half in range(2):
                    mms = []
                    for k in range(16):
                        mms.append(lambda e, h, s=s, half=half, k=k, buf=buf: e.matmul(
                            h[:, :], a_fm[:, k * MTOK + s * 128:k * MTOK + (s + 1) * 128],
                            w3b[buf][:, k * DM + half * 512:k * DM + (half + 1) * 512],
                            start=(k == 0), stop=(k == 15)))
                    sl, tv = fill(p, pool, mms, rd=("a_fm", "w3b"))
                    evac(p, "v", sl, tv, lambda e, h, s=s, half=half, e_=e_: e.scalar_tensor_tensor(
                        accf[:, s * DM + half * 512:s * DM + (half + 1) * 512],
                        h[:, :], cwf[:, s * NE + e_:s * NE + e_ + 1],
                        accf[:, s * DM + half * 512:s * DM + (half + 1) * 512],
                        ALU.mult, ALU.add),
                        rd=("cwfall", f"accf{s}"), wr=(f"accf{s}",))
                    if e_ == NE - 1 and half == 1:
                        p.op("y", lambda e, mg=mg, s=s: e.dma_start(
                            out=outd[mg * MTOK + s * 128:mg * MTOK + (s + 1) * 128, :],
                            in_=accf[:, s * DM:(s + 1) * DM]),
                            inc="dO", amt=16, rd=(f"accf{s}",))
                        if mg + 1 < MG:
                            x1_load(mg + 1, s)
            if gi + 1 < MG * NE:
                wload((gi + 1) % NE, (gi + 1) % 2)
    out_done = p.val("dO")

    p.op("y", lambda e: e.nop(), inc=None, waits=[("dO", out_done)])
    p.op("t", lambda e: e.nop(), inc=None, waits=[("dO", out_done)])

    r3.close()
    n_ops = sum(len(v) for v in p.ops.values())
    print(f"[build] ops: " + ", ".join(f"{k}:{len(v)}" for k, v in p.ops.items()) +
          f" total={n_ops}; sbuf left/part: {nc.sbuf_bytes_remaining}")

    def play(eng):
        def body(e):
            ops, phs = p.ops[eng], p.phases[eng]
            i = 0
            while i < len(ops):
                j = i
                while j < len(ops) and phs[j] == phs[i]:
                    j += 1
                with nc.named_scope(phs[i]):
                    for k in range(i, j):
                        ops[k](e)
                i = j
        return body

    with nc.Block() as block:
        block.tensor(play("t"))
        block.vector(play("v"))
        block.scalar(play("s"))
        block.gpsimd(play("g"))
        block.sync(play("y"))

    return nc


def make_in_maps(inputs, tpc=4096):
    g = lambda k: np.ascontiguousarray(np.asarray(inputs[k], dtype=np.float32))
    x = np.asarray(inputs["L0_attn_x"], dtype=np.float32)
    B, S, _ = x.shape
    qpb = S // tpc
    maps = []
    for cid in range(NCORES):
        b, q = cid // qpb, cid % qpb
        mask = np.zeros(NCORES, np.float32)
        for r in range(NCORES):
            rb, rq = r // qpb, r % qpb
            if rb == b and rq < q:
                mask[r] = 1.0
        maps.append(dict(
            x0=np.ascontiguousarray(x[b, q * tpc:(q + 1) * tpc, :]),
            qw=g("L0_attn_q_w"), kw=g("L0_attn_k_w"), vw=g("L0_attn_v_w"),
            ow=g("L0_attn_out_w"),
            qb=g("L0_attn_q_b"), kb=g("L0_attn_k_b"), vb=g("L0_attn_v_b"),
            ob=g("L0_attn_out_b"),
            proj=g("L0_attn_proj"), gamma=g("L0_attn_gamma"), beta=g("L0_attn_beta"),
            Wr=g("L0_ffn_Wr"), brr=g("L0_ffn_br"), b3=g("L0_ffn_b3"),
            w1e=np.ascontiguousarray(g("L0_ffn_W1")[cid]),
            w2e=np.ascontiguousarray(g("L0_ffn_W2")[cid]),
            b1e=np.ascontiguousarray(g("L0_ffn_b1")[cid]),
            b2e=np.ascontiguousarray(g("L0_ffn_b2")[cid]),
            w3e=np.ascontiguousarray(g("L0_ffn_W3")[cid]),
            pmask=np.repeat(mask, 128).astype(np.float32),
        ))
    return maps


def kernel(**inputs):
    tpc = 4096
    nc = build_program(tpc=tpc, spg=4)
    in_maps = make_in_maps(inputs, tpc=tpc)
    res = run_bass_kernel_spmd(nc, in_maps, core_ids=list(range(NCORES)))
    x = np.asarray(inputs["L0_attn_x"])
    B, S, _ = x.shape
    qpb = S // tpc
    out = np.empty((B, S, DM), np.float32)
    for cid in range(NCORES):
        b, q = cid // qpb, cid % qpb
        out[b, q * tpc:(q + 1) * tpc, :] = res.results[cid]["out"]
    return out


# version 18 (final)
# speedup vs baseline: 1.0417x; 1.0417x over previous
import contextlib

import numpy as np
import ml_dtypes
import concourse.bass as bass
import concourse.mybir as mybir
from concourse.bass_utils import run_bass_kernel_spmd

F32 = mybir.dt.float32
BF16 = mybir.dt.bfloat16
AF = mybir.ActivationFunctionType
ALU = mybir.AluOpType
AX = mybir.AxisListType

NCORES = 8
H, D, M, DM, HID, NE = 16, 64, 256, 1024, 2048, 8
KT = DM // 128
LNE1 = 1e-5
LNE2 = 1e-6
ST_W = H * (2 * D + 2)


class _Haz:
    __slots__ = ("w", "rs")

    def __init__(self):
        self.w = None
        self.rs = {}

    def rd_waits(self):
        return [self.w] if self.w else []

    def wr_waits(self):
        ws = dict(self.rs)
        if self.w:
            ws[self.w[0]] = max(ws.get(self.w[0], 0), self.w[1])
        return list(ws.items())

    def note_rd(self, sem, val):
        self.rs[sem] = max(self.rs.get(sem, 0), val)

    def note_wr(self, sem, val):
        self.w = (sem, val)
        self.rs = {}


class Prog:
    ENGS = ("t", "v", "s", "g", "y")

    def __init__(self, nc):
        self.nc = nc
        self.ops = {e: [] for e in self.ENGS}
        self.phases = {e: [] for e in self.ENGS}
        self.phase = "P0"
        self.sems = {}
        self.ctr = {}
        self.haz = {}
        self.known = {e: {} for e in self.ENGS}
        self.batch_open = {}

    def sem(self, name):
        self.sems[name] = self.nc.alloc_semaphore(name)
        self.ctr[name] = 0
        return name

    def val(self, name):
        return self.ctr[name]

    def hz(self, key):
        if key not in self.haz:
            self.haz[key] = _Haz()
        return self.haz[key]

    def barrier(self):
        snap = [(sn, v) for sn, v in self.ctr.items() if v > 0]
        for eng in self.ENGS:
            kn = self.known[eng]
            final = [(sn, v) for sn, v in snap if kn.get(sn, -1) < v]
            for sn, v in final:
                kn[sn] = v
            if not final:
                continue
            sems = self.sems

            def run(e, final=tuple(final)):
                for sn, v in final:
                    e.wait_ge(sems[sn], v)
                e.nop()
            self.ops[eng].append(run)
            self.phases[eng].append(self.phase)

    def op(self, eng, fn, inc, amt=1, rd=(), wr=(), wrpre=(), wrpost=(), waits=(),
           gate=None):
        wl = {}
        for sn, v in waits:
            if v > 0:
                wl[sn] = max(wl.get(sn, 0), v)
        for k in rd:
            for sn, v in self.hz(k).rd_waits():
                wl[sn] = max(wl.get(sn, 0), v)
        for k in tuple(wr) + tuple(wrpre):
            for sn, v in self.hz(k).wr_waits():
                wl[sn] = max(wl.get(sn, 0), v)
        if amt == 16 and inc is not None:
            opn = self.batch_open.get(inc, False)
            g_ = gate
            if g_ is None:
                if inc in ("dX", "dX1", "dW", "dC", "dC2", "dP", "dP2", "dL", "dL2"):
                    g_ = bool(wrpre or wr) or not opn
                else:
                    g_ = True
            if g_ and self.ctr[inc] > 0:
                wl[inc] = max(wl.get(inc, 0), self.ctr[inc])
            if g_:
                opn = True
            if wrpost or wr:
                opn = False
            self.batch_open[inc] = opn
        kn = self.known[eng]
        final = [(sn, v) for sn, v in wl.items() if kn.get(sn, -1) < v]
        for sn, v in final:
            kn[sn] = v
        assert inc is not None or (not rd and not wr and not wrpre and not wrpost), \
            "tracked op needs inc"
        if inc is not None:
            self.ctr[inc] += amt
            myv = self.ctr[inc]
            for k in rd:
                self.hz(k).note_rd(inc, myv)
            for k in tuple(wr) + tuple(wrpost):
                self.hz(k).note_wr(inc, myv)
        sems = self.sems

        def run(e, fn=fn, final=tuple(final), inc=inc, amt=amt):
            for sn, v in final:
                e.wait_ge(sems[sn], v)
            ins = fn(e)
            if inc is not None:
                ins.then_inc(sems[inc], amt)
        self.ops[eng].append(run)
        self.phases[eng].append(self.phase)
        return self.ctr[inc] if inc is not None else None


class BankPool:
    def __init__(self, slots):
        self.slots = [dict(h=h, free=None, key=f"_bank{i}") for i, h in enumerate(slots)]
        self.i = 0

    def acquire(self):
        s = self.slots[self.i % len(self.slots)]
        self.i += 1
        return s


def fill(p, pool, mms, rd=(), waits=()):
    s = pool.acquire()
    n = len(mms)
    tval = None
    for i, mm in enumerate(mms):
        if i == 0:
            w = list(waits)
            if s["free"] is not None:
                w.append(s["free"])
            for k in rd:
                for sn, v in p.hz(k).rd_waits():
                    w.append((sn, v))
            tval = p.op("t", (lambda e, mm=mm, h=s["h"]: mm(e, h)),
                        inc="sT" if n == 1 else None, waits=w)
        else:
            tval = p.op("t", (lambda e, mm=mm, h=s["h"]: mm(e, h)),
                        inc="sT" if i == n - 1 else None)
    for k in rd:
        p.hz(k).note_rd("sT", tval)
    s["tval"] = tval
    return s, tval


def fill_multi(p, pool, nslots, mms, rd=(), waits=()):
    slots = [pool.acquire() for _ in range(nslots)]
    n = len(mms)
    tval = None
    for i, mm in enumerate(mms):
        if i == 0:
            w = list(waits)
            for s in slots:
                if s["free"] is not None:
                    w.append(s["free"])
            for k in rd:
                for sn, v in p.hz(k).rd_waits():
                    w.append((sn, v))
            tval = p.op("t", (lambda e, mm=mm, hs=tuple(s["h"] for s in slots): mm(e, hs)),
                        inc="sT" if n == 1 else None, waits=w)
        else:
            tval = p.op("t", (lambda e, mm=mm, hs=tuple(s["h"] for s in slots): mm(e, hs)),
                        inc="sT" if i == n - 1 else None)
    for k in rd:
        p.hz(k).note_rd("sT", tval)
    for s in slots:
        s["tval"] = tval
    return slots, tval


def evac(p, eng, slot, tval, fn, rd=(), wr=(), wrpre=(), wrpost=(), waits=()):
    sem = {"v": "sV", "s": "sS"}[eng]
    w = [("sT", tval)] + list(waits)
    val = p.op(eng, (lambda e, fn=fn, h=slot["h"]: fn(e, h)),
               inc=sem, rd=rd, wr=wr, wrpre=wrpre, wrpost=wrpost, waits=w)
    slot["free"] = (sem, val)
    return val


def bf16v(h, col0, ncols):
    return h[:, col0:col0 + ncols // 2].bitcast(BF16)



def build_program(tpc=4096, spg=4):
    nc = bass.Bass("TRN2", target_bir_lowering=False, debug=False)
    SUBS = tpc // 128
    GRP = spg * 128
    NG = tpc // GRP
    MSUB = min(8, SUBS)
    MG = SUBS // MSUB
    MTOK = MSUB * 128

    ap = {}

    def din(name, shape, dt=F32):
        ap[name] = nc.dram_tensor(name, list(shape), dt, kind="ExternalInput").ap()
        return ap[name]

    x0d = din("x0", (tpc, DM))
    for w in ("qw", "kw", "vw", "ow"):
        din(w, (DM, DM))
    for b in ("qb", "kb", "vb", "ob"):
        din(b, (DM,))
    din("proj", (M, D))
    din("gamma", (DM,)); din("beta", (DM,))
    din("Wr", (DM, NE)); din("brr", (NE,)); din("b3", (NE, DM))
    din("w1e", (DM, HID)); din("w2e", (HID, HID))
    din("b1e", (HID,)); din("b2e", (HID,)); din("w3e", (HID, DM))
    din("pmask", (NCORES * 128,))
    outd = nc.dram_tensor("out", [tpc, DM], F32, kind="ExternalOutput").ap()

    ccin_s = nc.dram_tensor("ccin_s", [128, ST_W], F32, kind="Internal")
    ccout_s = nc.dram_tensor("ccout_s", [NCORES * 128, ST_W], F32, kind="Internal",
                             addr_space="Shared")
    WROWS = DM + 1 + HID // 2
    ccin_w = nc.dram_tensor("ccin_w", [WROWS, HID], BF16, kind="Internal")
    ccout_w = nc.dram_tensor("ccout_w", [NCORES * WROWS, HID], BF16, kind="Internal",
                             addr_space="Shared")
    x1scr = nc.dram_tensor("x1scr", [tpc, DM], F32, kind="Internal")
    KC_EQP, KC_KPT, KC_VTM = 0, 4096, 8192
    kcache = nc.dram_tensor("kcache", [tpc, 9216], BF16, kind="Internal")
    lncache = nc.dram_tensor("lncache", [tpc, 2], F32, kind="Internal")

    c_identb = nc.inline_tensor(np.eye(128, dtype=ml_dtypes.bfloat16), "c_identb")
    c_mask = nc.inline_tensor(np.triu(np.ones((128, 128), ml_dtypes.bfloat16)), "c_mask")
    c_onesb = nc.inline_tensor(np.ones((1, 128), ml_dtypes.bfloat16), "c_onesb")
    c_onescb = nc.inline_tensor(np.ones((128, 1), ml_dtypes.bfloat16), "c_onescb")
    c_onesf = nc.inline_tensor(np.ones((1, 128), np.float32), "c_onesf")
    olh = np.zeros((128, 2), np.float32)
    olh[0:64, 0] = 1.0
    olh[64:128, 1] = 1.0
    c_oneslh = nc.inline_tensor(olh, "c_oneslh")
    ckc_np = np.zeros((128, 2), np.float32)
    ckc_np[:, 0] = -1.0 / 128.0
    ckc_np[:, 1] = -np.log(16.0)
    c_ckc = nc.inline_tensor(ckc_np, "c_ckc")

    r2a = contextlib.ExitStack()
    r0 = contextlib.ExitStack()
    r1 = contextlib.ExitStack()
    r2b = contextlib.ExitStack()
    r3 = contextlib.ExitStack()

    def mk(stack, name, shape, dt):
        return stack.enter_context(nc.sbuf_tensor(name, shape, dt))

    class _Res:
        pass
    identb = nc.alloc_sbuf_tensor("identb", [128, 128], BF16)
    masktri = nc.alloc_sbuf_tensor("masktri", [128, 128], BF16)
    onesb = nc.alloc_sbuf_tensor("onesb", [1, 128], BF16)
    onescb = nc.alloc_sbuf_tensor("onescb", [128, 1], BF16)
    onesf = nc.alloc_sbuf_tensor("onesf", [1, 128], F32)
    brb = nc.alloc_sbuf_tensor("brb", [1, NE], BF16)
    b3b = nc.alloc_sbuf_tensor("b3b", [NE, DM], BF16)
    wrb = nc.alloc_sbuf_tensor("wrb", [128, KT * NE], BF16)
    b12f = nc.alloc_sbuf_tensor("b12f", [128, NE * 16], F32)
    pmv = nc.alloc_sbuf_tensor("pmv", [128, NCORES], F32)
    stat = nc.alloc_sbuf_tensor("stat", [128, 80], F32)
    ckcst = nc.alloc_sbuf_tensor("ckcst", [128, 2], F32)
    sqscr = nc.alloc_sbuf_tensor("sqscr", [128, 512], F32)
    sqscr2 = nc.alloc_sbuf_tensor("sqscr2", [128, 512], F32)

    wqb = mk(r2a, "wqb", [128, KT * DM], BF16)
    wkb = mk(r2a, "wkb", [128, KT * DM], BF16)
    wvb = mk(r2a, "wvb", [128, KT * DM], BF16)
    wob = mk(r2a, "wob", [128, KT * DM], BF16)
    projLo = mk(r2a, "projLo", [128, M], BF16)
    projHi = mk(r2a, "projHi", [128, M], BF16)
    qbt = mk(r2a, "qbt", [128, KT], F32)
    kbt = mk(r2a, "kbt", [128, KT], F32)
    oneslh = mk(r2a, "oneslh", [128, 2], F32)
    kbr = mk(r2a, "kbr", [1, DM], BF16)
    vbr = mk(r2a, "vbr", [1, DM], BF16)
    obr = mk(r2a, "obr", [1, DM], BF16)
    grow = mk(r2a, "grow", [1, DM], F32)
    brow = mk(r2a, "brow", [1, DM], F32)

    pool = BankPool([nc.alloc_psum_tensor(f"ps{i}", [128, 512], F32) for i in range(8)])

    p = Prog(nc)
    for s in ("sT", "sV", "sS", "sG", "dC", "dC2", "dP", "dX", "dW", "dX1", "dSt",
              "dO", "gC", "dG", "dK", "dL", "dL2", "dP2"):
        p.sem(s)

    def ldc(dst, src, wr, rd=(), slow=False):
        return p.op("y", lambda e, d=dst, s=src, slow=slow: e.dma_start(
            out=d, in_=s, allow_slow_non_contiguous=slow),
            inc="dC", amt=16, wr=wr, rd=rd)

    ldc(identb[:, :], c_identb.ap(), wr=("identb",))
    ldc(masktri[:, :], c_mask.ap(), wr=("masktri",))
    ldc(onesb[:, :], c_onesb.ap(), wr=("onesb",))
    ldc(onescb[:, :], c_onescb.ap(), wr=("onescb",))
    ldc(onesf[:, :], c_onesf.ap(), wr=("onesf",))
    ldc(pmv[:, :], ap["pmask"].rearrange("(r p) -> p r", p=128), wr=("pmv",), slow=True)
    ldc(qbt[:, :], ap["qb"].rearrange("(k p) -> p k", p=128), wr=("qbt",), slow=True)
    ldc(kbt[:, :], ap["kb"].rearrange("(k p) -> p k", p=128), wr=("kbt",), slow=True)
    ldc(oneslh[:, :], c_oneslh.ap(), wr=("oneslh",))
    ldc(ckcst[:, :], c_ckc.ap(), wr=("ckcst",))

    stg_w = [mk(r0, f"stg_w{i}", [128, KT * DM], F32) for i in range(2)]
    stg_r = mk(r0, "stg_r", [1, DM], F32)
    stg_p = mk(r0, "stg_p", [128, 2 * D], F32)
    stg_pb = mk(r0, "stg_pb", [128, 2 * D], BF16)
    stg_wr = mk(r0, "stg_wr", [128, KT * NE], F32)
    stg_b3 = mk(r0, "stg_b3", [NE, DM], F32)
    stg_g = mk(r0, "stg_g", [1, DM], F32)

    for wi, (name, dstb, key) in enumerate((("qw", wqb, "wqb"), ("kw", wkb, "wkb"),
                                            ("vw", wvb, "wvb"), ("ow", wob, "wob"))):
        par_ = wi % 2
        cls = "dC" if par_ == 0 else "dC2"
        for k in range(KT):
            kw = dict(wrpre=(f"stg_w{par_}",)) if k == 0 else {}
            if k == KT - 1:
                kw["wrpost"] = (f"stg_w{par_}",)
            p.op("y", lambda e, name=name, k=k, par_=par_: e.dma_start(
                out=stg_w[par_][:, k * DM:(k + 1) * DM],
                in_=ap[name][k * 128:(k + 1) * 128, :]),
                inc=cls, amt=16, **kw)
        p.op("v", lambda e, d=dstb, par_=par_: e.tensor_copy(d[:, :], stg_w[par_][:, :]),
             inc="sV", rd=(f"stg_w{par_}",), wr=(key,))
    for name, dst, key in (("kb", kbr, "kbr"), ("vb", vbr, "vbr"), ("ob", obr, "obr")):
        ldc(stg_r[:, :], ap[name].rearrange("(one f) -> one f", one=1), wr=("stg_r",))
        p.op("v", lambda e, d=dst: e.tensor_copy(d[:, :], stg_r[:, :]),
             inc="sV", rd=("stg_r",), wr=(key,))
    ldc(stg_r[:, 0:NE], ap["brr"].rearrange("(one f) -> one f", one=1), wr=("stg_r",))
    p.op("v", lambda e: e.tensor_copy(brb[:, :], stg_r[:, 0:NE]),
         inc="sV", rd=("stg_r",), wr=("brb",))
    ldc(stg_b3[:, :], ap["b3"], wr=("stg_b3",))
    p.op("v", lambda e: e.tensor_copy(b3b[:, :], stg_b3[:, :]),
         inc="sV", rd=("stg_b3",), wr=("b3b",))
    for k in range(KT):
        kw = dict(wrpre=("stg_wr",)) if k == 0 else {}
        if k == KT - 1:
            kw["wrpost"] = ("stg_wr",)
        p.op("y", lambda e, k=k: e.dma_start(
            out=stg_wr[:, k * NE:(k + 1) * NE], in_=ap["Wr"][k * 128:(k + 1) * 128, :]),
            inc="dC", amt=16, **kw)
    p.op("v", lambda e: e.tensor_copy(wrb[:, :], stg_wr[:, :]),
         inc="sV", rd=("stg_wr",), wr=("wrb",))
    ldc(stg_p[:, 0:D], ap["proj"][0:128, :], wr=())
    ldc(stg_p[:, D:2 * D], ap["proj"][128:256, :], wr=("stg_p",))
    p.haz["stg_p"].w = ("dC", p.val("dC"))
    p.op("v", lambda e: e.tensor_scalar(stg_pb[:, :], stg_p[:, :], 0.125, None, ALU.mult),
         inc="sV", rd=("stg_p",), wr=("stg_pb",), waits=[("dC", p.val("dC"))])
    p.op("v", lambda e: e.memset(projLo[64:128, :], 0.0), inc="sV", wrpre=("projLo",))
    pmz = p.op("v", lambda e: e.memset(projHi[0:64, :], 0.0), inc="sV", wrpre=("projHi",))
    for j in range(2):
        sl, tv = fill(p, pool, [lambda e, h, j=j: e.transpose(
            h[0:64, 0:64].bitcast(BF16), stg_pb[:, j * D:(j + 1) * D], identb[:, :])],
            rd=("stg_pb", "identb"))
        evac(p, "v", sl, tv, lambda e, h, j=j: e.tensor_copy(
            projLo[0:64, j * 128:(j + 1) * 128], h[0:64, 0:64].bitcast(BF16)),
            **({"wrpost": ("projLo",)} if j == 1 else {}))
    p.op("y", lambda e: e.dma_start(out=projHi[64:128, :], in_=projLo[0:64, :]),
         inc="dC", amt=16, rd=("projLo",), wrpost=("projHi",), waits=[("sV", pmz)])
    ldc(grow[:, :], ap["gamma"].rearrange("(one f) -> one f", one=1), wr=("grow",))
    ldc(brow[:, :], ap["beta"].rearrange("(one f) -> one f", one=1), wr=("brow",))

    p.phase = "P1"
    p.barrier()
    r0.close()
    w1t = mk(r1, "w1t", [128, 16 * DM], BF16)
    w2kb = mk(r1, "w2kb", [128, 16 * HID], BF16)
    stg1 = [mk(r1, f"stg1_{i}", [128, HID], F32) for i in range(2)]
    stg1b = mk(r1, "stg1b", [128, HID], BF16)
    b1b = mk(r1, "b1b", [128, 16], BF16)
    stgb2 = mk(r1, "stgb2", [1, 512], F32)
    stgb12 = mk(r1, "stgb12", [1, 512], F32)

    def ldp(dst, src, wr, rd=()):
        return p.op("y", lambda e, d=dst, s=src: e.dma_start(out=d, in_=s),
                    inc="dP", amt=16, wr=wr, rd=rd)

    p.op("y", lambda e: e.dma_start(out=stg1[0][:, 0:16],
        in_=ap["b1e"].rearrange("(k p) -> p k", p=128),
        allow_slow_non_contiguous=True), inc="dP", amt=16, wr=("stg1_0",))
    p.op("v", lambda e: e.tensor_copy(b1b[:, :], stg1[0][:, 0:16]),
         inc="sV", rd=("stg1_0",), wr=("b1b",))

    for dt_ in range(KT):
        pr1 = dt_ % 2
        p.op("y", lambda e, dt_=dt_, pr1=pr1: e.dma_start(
            out=stg1[pr1][:, :], in_=ap["w1e"][dt_ * 128:(dt_ + 1) * 128, :]),
            inc="dP" if pr1 == 0 else "dP2", amt=16, wr=(f"stg1_{pr1}",))
        p.op("v", lambda e, pr1=pr1: e.tensor_copy(stg1b[:, :], stg1[pr1][:, :]),
             inc="sV", rd=(f"stg1_{pr1}",), wr=("stg1b",))
        for j in range(16):
            sl, tv = fill(p, pool, [lambda e, h, j=j: e.transpose(
                bf16v(h, 0, 128), stg1b[:, j * 128:(j + 1) * 128], identb[:, :])],
                rd=("stg1b", "identb"))
            evac(p, "v", sl, tv, lambda e, h, j=j, dt_=dt_: e.tensor_copy(
                w1t[:, j * DM + dt_ * 128:j * DM + (dt_ + 1) * 128], bf16v(h, 0, 128)),
                wr=(f"w1t{j}_{dt_}",))
    for k in range(16):
        pr1 = k % 2
        p.op("y", lambda e, k=k, pr1=pr1: e.dma_start(
            out=stg1[pr1][:, :], in_=ap["w2e"][k * 128:(k + 1) * 128, :]),
            inc="dP" if pr1 == 0 else "dP2", amt=16, wr=(f"stg1_{pr1}",))
        p.op("v", lambda e, k=k, pr1=pr1: e.tensor_copy(
            w2kb[:, k * HID:(k + 1) * HID], stg1[pr1][:, :]),
            inc="sV", rd=(f"stg1_{pr1}",), wr=(f"w2kb{k}",))
    for k in range(16):
        pr1 = k % 2
        p.op("y", lambda e, k=k, pr1=pr1: e.dma_start(
            out=stg1[pr1][:, 0:DM], in_=ap["w3e"][k * 128:(k + 1) * 128, :]),
            inc="dP" if pr1 == 0 else "dP2", amt=16, wr=(f"stg1_{pr1}",))
        p.op("v", lambda e, pr1=pr1: e.tensor_copy(stg1b[:, 0:DM], stg1[pr1][:, 0:DM]),
             inc="sV", rd=(f"stg1_{pr1}",), wr=("stg1b",))
        p.op("s", lambda e, k=k: e.dma_start(
            out=ccin_w.ap()[DM + 1 + k * 64:DM + 1 + (k + 1) * 64, :]
                .rearrange("a (b c) -> (a b) c", b=2),
            in_=stg1b[:, 0:DM]), inc="dSt", amt=16, rd=("stg1b",))

    w12rd = tuple([f"w1t{jg}_{dt_}" for jg in range(2) for dt_ in range(KT)] +
                  [f"w2kb{k}" for k in range(16)])
    for mt in range(KT):
        for q4 in range(4):
            mms = []
            for k in range(16):
                mms.append(lambda e, h, mt=mt, q4=q4, k=k: e.matmul(
                    h[:, :], w1t[:, k * DM + mt * 128:k * DM + (mt + 1) * 128],
                    w2kb[:, k * HID + q4 * 512:k * HID + (q4 + 1) * 512],
                    start=(k == 0), stop=(k == 15)))
            sl, tv = fill(p, pool, mms, rd=w12rd if (mt == 0 and q4 == 0) else ())
            evac(p, "v", sl, tv, lambda e, h, q4=q4: e.tensor_copy(
                stg1b[:, q4 * 512:(q4 + 1) * 512], h[:, :]),
                **(dict(wrpre=("stg1b",)) if q4 == 0 else
                   (dict(wrpost=("stg1b",)) if q4 == 3 else {})))
        p.op("s", lambda e, mt=mt: e.dma_start(
            out=ccin_w.ap()[mt * 128:(mt + 1) * 128, :], in_=stg1b[:, :]),
            inc="dSt", amt=16, rd=("stg1b",))
    for q4 in range(4):
        ldp(stgb2[:, :], ap["b2e"][q4 * 512:(q4 + 1) * 512]
            .rearrange("(one f) -> one f", one=1), wr=("stgb2",))
        mms = []
        for k in range(16):
            mms.append(lambda e, h, q4=q4, k=k: e.matmul(
                h[0:1, :], b1b[:, k:k + 1],
                w2kb[:, k * HID + q4 * 512:k * HID + (q4 + 1) * 512],
                start=(k == 0), stop=(k == 15)))
        sl, tv = fill(p, pool, mms, rd=("b1b",) if q4 == 0 else ())
        evac(p, "v", sl, tv, lambda e, h, q4=q4: e.tensor_copy(
            stgb12[:, :], h[0:1, :]), wr=("stgb12",))
        p.op("v", lambda e, q4=q4: e.tensor_tensor(
            stg1b[0:1, q4 * 512:(q4 + 1) * 512], stgb12[:, :], stgb2[:, :], ALU.add),
            inc="sV", rd=("stgb12", "stgb2"),
            **(dict(wrpre=("stg1b",)) if q4 == 0 else
               (dict(wrpost=("stg1b",)) if q4 == 3 else {})))
    p.op("s", lambda e: e.dma_start(out=ccin_w.ap()[DM:DM + 1, :], in_=stg1b[0:1, :]),
         inc="dSt", amt=16, rd=("stg1b",))
    w12_stores_done = p.val("dSt")

    p.barrier()
    r1.close()
    state = mk(r2b, "state", [128, ST_W], F32)
    stateb = mk(r2b, "stateb", [128, ST_W], BF16)
    x0g = [mk(r2b, f"x0g{i}", [128, spg * DM], F32) for i in range(2)]
    ln1m = mk(r2b, "ln1m", [128, 2 * spg], F32)
    ln1r = mk(r2b, "ln1r", [128, 2 * spg], F32)
    z1bA = [mk(r2b, f"z1b{i}", [128, DM], BF16) for i in range(2)]
    xn1f = mk(r2b, "xn1f", [128, KT * GRP], BF16)
    qfm = mk(r2b, "qfm", [128, KT * GRP], BF16)
    kfm = mk(r2b, "kfm", [128, DM], BF16)
    vtm = mk(r2b, "vtm", [128, DM], BF16)
    sqfm = mk(r2b, "sqfm", [128, DM], F32)
    ckf = mk(r2b, "ckf", [128, H], F32)
    eqp = mk(r2b, "eqp", [128, H * 2 * 128], BF16)
    ekp = mk(r2b, "ekp", [128, H * 2 * 128], BF16)
    kpt = mk(r2b, "kpt", [128, H * M], BF16)
    amb = [mk(r2b, f"amb{i}", [128, 256], BF16) for i in range(2)]
    rcol = mk(r2b, "rcol", [128, 2], F32)
    atm = mk(r2b, "atm", [128, DM], BF16)
    afm = mk(r2b, "afm", [128, DM], BF16)
    resf = mk(r2b, "resf", [128, DM], F32)
    scr2 = mk(r2b, "scr2", [128, DM], F32)
    stS = mk(r2b, "stS", [128, ST_W // 4], F32)
    lnx = mk(r2b, "lnx", [128, 2], F32)
    gammaB = mk(r2b, "gammaB", [128, DM], F32)
    betaB = mk(r2b, "betaB", [128, DM], F32)
    for srow, dst, key in ((grow, gammaB, "gammaB"), (brow, betaB, "betaB")):
        for half in range(2):
            sl, tv = fill(p, pool, [lambda e, h, half=half, srow=srow: e.matmul(
                h[:, :], onesf[0:1, :], srow[0:1, half * 512:(half + 1) * 512],
                start=True, stop=True)], rd=("onesf", "grow", "brow"))
            evac(p, "v", sl, tv, lambda e, h, d=dst, half=half: e.tensor_copy(
                d[:, half * 512:(half + 1) * 512], h[:, :]),
                **(dict(wrpre=(key,)) if half == 0 else dict(wrpost=(key,))))

    def emit_ln_stats(src_key, src_ap, eps):
        p.op("v", lambda e: e.reduce_sum(stat[:, 40:41], src_ap, axis=AX.X),
             inc="sV", rd=(src_key,), wr=("stat2",))
        p.op("s", lambda e: e.activation(sqscr[:, :], src_ap[:, 0:512], AF.Square,
                                         accum_out=stat[:, 41:42]),
             inc="sS", rd=(src_key,), wr=("stat2", "sqscr"))
        p.op("s", lambda e: e.activation(sqscr2[:, :], src_ap[:, 512:1024], AF.Square,
                                         accum_out=stat[:, 42:43]),
             inc="sS", rd=(src_key,), wr=("stat2", "sqscr2"))
        p.op("v", lambda e: e.tensor_tensor(stat[:, 41:42], stat[:, 41:42],
                                            stat[:, 42:43], ALU.add),
             inc="sV", rd=("stat2",), wr=("stat2",))
        p.op("v", lambda e: e.tensor_scalar(stat[:, 43:44], stat[:, 40:41], 1.0 / DM,
                                            None, ALU.mult),
             inc="sV", rd=("stat2",), wr=("stat2",))
        p.op("v", lambda e: e.tensor_tensor(stat[:, 44:45], stat[:, 43:44],
                                            stat[:, 43:44], ALU.mult),
             inc="sV", rd=("stat2",), wr=("stat2",))
        p.op("v", lambda e: e.tensor_scalar(stat[:, 41:42], stat[:, 41:42], 1.0 / DM,
                                            None, ALU.mult),
             inc="sV", rd=("stat2",), wr=("stat2",))
        p.op("v", lambda e: e.tensor_tensor(stat[:, 45:46], stat[:, 41:42],
                                            stat[:, 44:45], ALU.subtract),
             inc="sV", rd=("stat2",), wr=("stat2",))
        p.op("v", lambda e, eps=eps: e.tensor_scalar(stat[:, 45:46], stat[:, 45:46],
                                                     eps, None, ALU.add),
             inc="sV", rd=("stat2",), wr=("stat2",))
        p.op("v", lambda e: e.reciprocal(stat[:, 46:47], stat[:, 45:46]),
             inc="sV", rd=("stat2",), wr=("stat2",))
        sqv = p.op("s", lambda e: e.activation(stat[:, 48:49], stat[:, 46:47], AF.Sqrt),
                   inc="sS", rd=("stat2",), wr=("stat2s",))
        p.op("v", lambda e: e.tensor_scalar(stat[:, 47:48], stat[:, 43:44], -1.0,
                                            None, ALU.mult),
             inc="sV", rd=("stat2",), wr=("stat2",), waits=[("sS", sqv)])

    def emit_ln_stats_group(srcs, eps, mdst, rdst, gkey):
        n = len(srcs)
        for i, (k, apx) in enumerate(srcs):
            p.op("v", lambda e, i=i, apx=apx: e.reduce_sum(
                stat[:, i:i + 1], apx, axis=AX.X),
                inc="sV", rd=(k,),
                wrpre=("statS",) if i == 0 else (),
                wrpost=("statS",) if i == n - 1 else ())
        for i, (k, apx) in enumerate(srcs):
            for half in range(2):
                dump = sqscr if (2 * i + half) % 2 == 0 else sqscr2
                dk = "sqscr" if (2 * i + half) % 2 == 0 else "sqscr2"
                col = 8 + half * 8 + i
                first = (i == 0 and half == 0)
                last = (i == n - 1 and half == 1)
                p.op("s", lambda e, apx=apx, half=half, col=col, dump=dump:
                     e.activation(dump[:, :], apx[:, half * 512:(half + 1) * 512],
                                  AF.Square, accum_out=stat[:, col:col + 1]),
                     inc="sS", rd=(k,), wr=(dk,),
                     wrpre=("statQ",) if first else (),
                     wrpost=("statQ",) if last else ())
        p.op("v", lambda e, n=n: e.tensor_tensor(
            stat[:, 8:8 + n], stat[:, 8:8 + n], stat[:, 16:16 + n], ALU.add),
            inc="sV", rd=("statQ",), wr=("statQ",))
        p.op("v", lambda e, n=n: e.tensor_scalar(
            stat[:, 24:24 + n], stat[:, 0:n], 1.0 / DM, None, ALU.mult),
            inc="sV", rd=("statS",), wr=("statM",))
        p.op("v", lambda e, n=n: e.tensor_tensor(
            stat[:, 32:32 + n], stat[:, 24:24 + n], stat[:, 24:24 + n], ALU.mult),
            inc="sV", rd=("statM",), wr=("statV",))
        p.op("v", lambda e, n=n: e.tensor_scalar(
            stat[:, 8:8 + n], stat[:, 8:8 + n], 1.0 / DM, None, ALU.mult),
            inc="sV", rd=("statQ",), wr=("statQ",))
        p.op("v", lambda e, n=n: e.tensor_tensor(
            stat[:, 8:8 + n], stat[:, 8:8 + n], stat[:, 32:32 + n], ALU.subtract),
            inc="sV", rd=("statQ", "statV"), wr=("statQ",))
        p.op("v", lambda e, n=n, eps=eps: e.tensor_scalar(
            stat[:, 8:8 + n], stat[:, 8:8 + n], eps, None, ALU.add),
            inc="sV", rd=("statQ",), wr=("statQ",))
        p.op("v", lambda e, n=n: e.reciprocal(stat[:, 64:64 + n], stat[:, 8:8 + n]),
            inc="sV", rd=("statQ",), wr=("statR",))
        sqv = p.op("s", lambda e, n=n, rdst=rdst: e.activation(
            rdst[:, 0:n], stat[:, 64:64 + n], AF.Sqrt),
            inc="sS", rd=("statR",), wrpre=(gkey,))
        p.op("v", lambda e, n=n, mdst=mdst: e.tensor_scalar(
            mdst[:, 0:n], stat[:, 24:24 + n], -1.0, None, ALU.mult),
            inc="sV", rd=("statM",), wrpost=(gkey,), waits=[("sS", sqv)])

    def emit_xn1_transpose(s, first_sub=True, last_sub=True):
        zb, zk = z1bA[s % 2], f"z1b{s % 2}"
        mms = [lambda e, h, k=k, zb=zb: e.transpose(
            h[:, k * 64:(k + 1) * 64].bitcast(BF16),
            zb[:, k * 128:(k + 1) * 128], identb[:, :]) for k in range(KT)]
        sl, tv = fill(p, pool, mms, rd=(zk, "identb"))
        dst = xn1f[:, :].rearrange("p (k g) -> p k g", k=KT)[:, :, s * 128:(s + 1) * 128]
        evac(p, "v", sl, tv, lambda e, h, dst=dst: e.tensor_copy(
            dst, h[:, :].bitcast(BF16).rearrange("p (k g) -> p k g", k=KT)),
            wrpre=("xn1f",) if first_sub else (),
            wrpost=("xn1f",) if last_sub else ())

    def emit_kv_sub(s):
        for half in range(2):
            mms = []
            for k in range(KT):
                mms.append(lambda e, h, k=k, half=half, s=s: e.matmul(
                    h[:, :], xn1f[:, k * GRP + s * 128:k * GRP + (s + 1) * 128],
                    wvb[:, k * DM + half * 512:k * DM + (half + 1) * 512],
                    start=(k == 0), stop=False))
            mms.append(lambda e, h, half=half: e.matmul(
                h[:, :], onesb[0:1, :], vbr[0:1, half * 512:(half + 1) * 512],
                start=False, stop=True))
            sl, tv = fill(p, pool, mms, rd=("xn1f", "wvb", "onesb", "vbr"))
            evac(p, "v", sl, tv, lambda e, h, half=half: e.tensor_copy(
                vtm[:, half * 512:(half + 1) * 512], h[:, :]),
                wr=(f"vtmc{2 * half}", f"vtmc{2 * half + 1}"))
        for o in range(KT):
            mms = []
            for k in range(KT):
                mms.append(lambda e, h, o=o, k=k, s=s: e.matmul(
                    h[:, 0:128], wkb[:, k * DM + o * 128:k * DM + (o + 1) * 128],
                    xn1f[:, k * GRP + s * 128:k * GRP + (s + 1) * 128],
                    start=(k == 0), stop=(k == KT - 1)))
            sl, tv = fill(p, pool, mms, rd=("xn1f", "wkb"))
            kw = dict(wrpre=("kfm",)) if o == 0 else {}
            if o == KT - 1:
                kw = dict(wr=("kfm",))
            evac(p, "v", sl, tv, lambda e, h, o=o: e.tensor_scalar(
                kfm[:, o * 128:(o + 1) * 128], h[:, 0:128], kbt[:, o:o + 1],
                None, ALU.add), rd=("kbt",), **kw)
        for o in range(KT):
            kw = dict(wrpre=("sqfm",)) if o == 0 else {}
            if o == KT - 1:
                kw = dict(wrpost=("sqfm",))
            p.op("s", lambda e, o=o: e.activation(
                sqfm[:, o * 128:(o + 1) * 128], kfm[:, o * 128:(o + 1) * 128],
                AF.Square), inc="sS", rd=("kfm",), **kw)
        mms = []
        for h_ in range(H):
            mms.append(lambda e, h, h_=h_: e.matmul(
                h[:, h_:h_ + 1], sqfm[:, (h_ // 2) * 128:(h_ // 2 + 1) * 128],
                oneslh[:, (h_ % 2):(h_ % 2) + 1],
                start=(h_ == 0), stop=(h_ == H - 1), skip_group_check=True))
        sl, tv = fill(p, pool, mms, rd=("sqfm", "oneslh"))
        evac(p, "v", sl, tv, lambda e, h: e.tensor_scalar(
            ckf[:, :], h[:, 0:H], -1.0 / 128.0, -float(np.log(16.0)),
            ALU.mult, ALU.add), wr=("ckf",))
        for t4 in range(KT):
            mms = [lambda e, hs, t4=t4, pi=pi: e.matmul(
                hs[pi][:, 0:M], kfm[:, t4 * 128:(t4 + 1) * 128],
                (projLo if pi == 0 else projHi)[:, :],
                start=True, stop=True) for pi in range(2)]
            sls, tv = fill_multi(p, pool, 2, mms, rd=("kfm", "projLo", "projHi"))
            for pi in range(2):
                h_ = t4 * 2 + pi
                kw = dict(wrpre=(f"kptc{h_ // 4}",)) if h_ % 4 == 0 else {}
                if h_ % 4 == 3:
                    kw = dict(wrpost=(f"kptc{h_ // 4}",))
                evac(p, "s", sls[pi], tv, lambda e, h, h_=h_: e.activation(
                    kpt[:, h_ * M:(h_ + 1) * M], h[:, 0:M], AF.Exp,
                    bias=ckf[:, h_:h_ + 1]), rd=("ckf",), **kw)

    def emit_q_and_maps(s):
        for h_ in range(H):
            pv = projLo if h_ % 2 == 0 else projHi
            pk = "projLo" if h_ % 2 == 0 else "projHi"
            mms = []
            for j in range(2):
                mms.append(lambda e, h, h_=h_, j=j, s=s, pv=pv: e.matmul(
                    h[:, j * 128:(j + 1) * 128], pv[:, j * 128:(j + 1) * 128],
                    qfm[:, (h_ // 2) * GRP + s * 128:(h_ // 2) * GRP + (s + 1) * 128],
                    start=True, stop=True))
            sl, tv = fill(p, pool, mms, rd=("qfm", pk))
            kw = dict(wrpre=(f"eqpc{h_ // 4}",)) if h_ % 4 == 0 else {}
            if h_ % 4 == 3:
                kw = dict(wrpost=(f"eqpc{h_ // 4}",))
            evac(p, "s", sl, tv, lambda e, h, h_=h_: e.activation(
                eqp[:, h_ * 256:(h_ + 1) * 256], h[:, 0:256], AF.Exp), **kw)

    def emit_state_update(h_, phase_c):
        mms = []
        for j in range(2):
            mms.append(lambda e, h, h_=h_, j=j: e.matmul(
                h[:, j * 64:(j + 1) * 64], kpt[:, h_ * M + j * 128:h_ * M + (j + 1) * 128],
                vtm[:, h_ * D:(h_ + 1) * D], start=True, stop=True))
        for j in range(2):
            mms.append(lambda e, h, h_=h_, j=j: e.matmul(
                h[:, 128 + j:129 + j], kpt[:, h_ * M + j * 128:h_ * M + (j + 1) * 128],
                onescb[:, :], start=True, stop=True))
        sl, tv = fill(p, pool, mms,
                      rd=(f"kptc{h_ // 4}", f"vtmc{h_ // 4}", "onescb"))
        c0 = h_ * (2 * D + 2)
        evac(p, "v", sl, tv, lambda e, h, c0=c0: e.tensor_tensor(
            state[:, c0:c0 + 130], state[:, c0:c0 + 130], h[:, 0:130], ALU.add),
            rd=(f"st{h_}",), wr=(f"st{h_}",))
        if phase_c:
            p.op("s", lambda e, c0=c0: e.activation(
                stateb[:, c0:c0 + 130], state[:, c0:c0 + 130], AF.Copy),
                inc="sS", rd=(f"st{h_}",), wr=(f"stb{h_}",))

    p.phase = "A"
    p.op("v", lambda e: e.memset(state[:, :], 0.0), inc="sV",
         wr=tuple(f"st{h_}" for h_ in range(H)))

    def x0_load(g, par):
        for s in range(spg):
            kw = dict(wrpre=(f"x0g{par}",)) if s == 0 else {}
            if s == spg - 1:
                kw["wrpost"] = (f"x0g{par}",)
            p.op("y", lambda e, g=g, s=s, par=par: e.dma_start(
                out=x0g[par][:, s * DM:(s + 1) * DM],
                in_=x0d[g * GRP + s * 128:g * GRP + (s + 1) * 128, :]),
                inc="dX", amt=16, **kw)
        return p.val("dX")

    def ln1_group(g):
        par = g % 2
        emit_ln_stats_group(
            [(f"x0g{par}", x0g[par][:, s * DM:(s + 1) * DM]) for s in range(spg)],
            LNE1, ln1m[:, par * spg:], ln1r[:, par * spg:], f"ln1g{par}")

    wag_done = [False]

    def emit_wag():
        wag_done[0] = True
        p.op("g", lambda e: e.collective_compute(
            "AllGather", ALU.bypass, replica_groups=[list(range(NCORES))],
            ins=[ccin_w.ap().opt()], outs=[ccout_w.ap().opt()]),
            inc="gC", waits=[("dSt", w12_stores_done)])

    xv = {0: x0_load(0, 0)}
    for g in range(NG):
        par = g % 2
        if g == 1:
            emit_wag()
        if g == 0:
            ln1_group(0)
        if g + 1 < NG:
            xv[g + 1] = x0_load(g + 1, (g + 1) % 2)
            ln1_group(g + 1)
        for s in range(spg):
            stm = par * spg + s
            p.op("v", lambda e, s=s, par=par, stm=stm: e.tensor_scalar(
                z1bA[s % 2][:, :], x0g[par][:, s * DM:(s + 1) * DM],
                ln1m[:, stm:stm + 1],
                ln1r[:, stm:stm + 1], ALU.add, ALU.mult),
                inc="sV", rd=(f"x0g{par}", f"ln1g{par}"), wr=(f"z1b{s % 2}",))
            emit_xn1_transpose(s)
            p.op("g", lambda e, g=g, s=s, stm=stm: e.dma_start(
                out=lncache.ap()[g * GRP + s * 128:g * GRP + (s + 1) * 128, 0:1],
                in_=ln1m[:, stm:stm + 1], allow_slow_non_contiguous=True),
                inc="dK", amt=16, rd=(f"ln1g{par}",))
            p.op("g", lambda e, g=g, s=s, stm=stm: e.dma_start(
                out=lncache.ap()[g * GRP + s * 128:g * GRP + (s + 1) * 128, 1:2],
                in_=ln1r[:, stm:stm + 1], allow_slow_non_contiguous=True),
                inc="dK", amt=16, rd=(f"ln1g{par}",))
        for o in range(KT):
            mms = []
            for k in range(KT):
                mms.append(lambda e, h, o=o, k=k: e.matmul(
                    h[:, 0:GRP], wqb[:, k * DM + o * 128:k * DM + (o + 1) * 128],
                    xn1f[:, k * GRP:(k + 1) * GRP],
                    start=(k == 0), stop=(k == KT - 1)))
            sl, tv = fill(p, pool, mms, rd=("xn1f", "wqb"))
            kw = dict(wrpre=("qfm",)) if o == 0 else {}
            if o == KT - 1:
                kw = dict(wr=("qfm",))
            evac(p, "v", sl, tv, lambda e, h, o=o: e.tensor_scalar(
                qfm[:, o * GRP:(o + 1) * GRP], h[:, 0:GRP], qbt[:, o:o + 1],
                None, ALU.add), rd=("qbt",), **kw)
        for s in range(spg):
            row0 = g * GRP + s * 128
            emit_kv_sub(s)
            emit_q_and_maps(s)
            p.op("g", lambda e, row0=row0: e.dma_start(
                out=kcache.ap()[row0:row0 + 128, KC_EQP:KC_EQP + 4096],
                in_=eqp[:, :]), inc="dK", amt=16,
                rd=tuple(f"eqpc{c}" for c in range(4)))
            p.op("g", lambda e, row0=row0: e.dma_start(
                out=kcache.ap()[row0:row0 + 128, KC_KPT:KC_KPT + 4096],
                in_=kpt[:, :]), inc="dK", amt=16,
                rd=tuple(f"kptc{c}" for c in range(4)))
            p.op("g", lambda e, row0=row0: e.dma_start(
                out=kcache.ap()[row0:row0 + 128, KC_VTM:KC_VTM + 1024],
                in_=vtm[:, :]), inc="dK", amt=16,
                rd=tuple(f"vtmc{c}" for c in range(4)))
            for h_ in range(H):
                emit_state_update(h_, phase_c=False)

    p.phase = "CC"
    if not wag_done[0]:
        emit_wag()
    p.op("s", lambda e: e.dma_start(out=ccin_s.ap(), in_=state[:, :]),
         inc="dSt", amt=16, rd=tuple(f"st{h_}" for h_ in range(H)))
    st_staged = p.val("dSt")
    p.op("g", lambda e: e.collective_compute(
        "AllGather", ALU.bypass, replica_groups=[list(range(NCORES))],
        ins=[ccin_s.ap().opt()], outs=[ccout_s.ap().opt()]),
        inc="gC", waits=[("dSt", st_staged)])
    HW2 = ST_W // 4
    stg_b12 = mk(r2b, "stg_b12", [128, 16], BF16)

    def emit_state_init():
        for half in range(4):
            for r in range(NCORES):
                hkeys = tuple(f"st{h_}" for h_ in range(half * 4, half * 4 + 4))
                p.op("y", lambda e, r=r, half=half: e.dma_start(
                    out=stS[:, :],
                    in_=ccout_s.ap()[r * 128:(r + 1) * 128,
                                     half * HW2:(half + 1) * HW2]),
                    inc="dG", amt=16, wr=("stS",), waits=[("gC", 2)])
                if r == 0:
                    p.op("v", lambda e, half=half: e.tensor_scalar(
                        state[:, half * HW2:(half + 1) * HW2], stS[:, :], pmv[:, 0:1],
                        None, ALU.mult),
                        inc="sV", rd=("stS", "pmv"), wr=hkeys)
                else:
                    p.op("v", lambda e, r=r, half=half: e.scalar_tensor_tensor(
                        state[:, half * HW2:(half + 1) * HW2], stS[:, :],
                        pmv[:, r:r + 1],
                        state[:, half * HW2:(half + 1) * HW2], ALU.mult, ALU.add),
                        inc="sV", rd=("stS", "pmv") + hkeys, wr=hkeys)
            p.op("v", lambda e, half=half: e.tensor_copy(
                stateb[:, half * HW2:(half + 1) * HW2],
                state[:, half * HW2:(half + 1) * HW2]),
                inc="sV",
                rd=tuple(f"st{h_}" for h_ in range(half * 4, half * 4 + 4)),
                wr=tuple(f"stb{h_}" for h_ in range(half * 4, half * 4 + 4)))

    def emit_b12_load():
        for e_ in range(NE):
            p.op("y", lambda e, e_=e_: e.dma_start(
                out=stg_b12[:, :],
                in_=ccout_w.ap()[e_ * WROWS + DM, :].rearrange("(k p) -> p k", p=128),
                allow_slow_non_contiguous=True),
                inc="dG", amt=16, wr=("stg_b12",), waits=[("gC", 1)])
            p.op("v", lambda e, e_=e_: e.tensor_copy(b12f[:, e_ * 16:(e_ + 1) * 16],
                                                     stg_b12[:, :]),
                 inc="sV", rd=("stg_b12",), wr=("b12f",) if e_ == 0 else ())

    p.phase = "C"
    x1_store_vals = []
    xv = {0: x0_load(0, 0)}
    for g in range(NG):
        par = g % 2
        if g + 1 < NG:
            xv[g + 1] = x0_load(g + 1, (g + 1) % 2)
        if g == min(1, NG - 1):
            emit_b12_load()
        if g == 0:
            emit_state_init()
        for s in range(spg):
            row0 = g * GRP + s * 128
            nld = 0
            lnx_w = []
            for ci in range(4):
                for name, base, width, key in (
                        ("eqp", KC_EQP, 1024, f"eqpc{ci}"),
                        ("kpt", KC_KPT, 1024, f"kptc{ci}"),
                        ("vtm", KC_VTM, 256, f"vtmc{ci}")):
                    dst = {"eqp": eqp, "kpt": kpt, "vtm": vtm}[name]
                    cls = "dL" if nld % 2 == 0 else "dL2"
                    p.op("y", lambda e, dst=dst, ci=ci, base=base, width=width,
                         row0=row0: e.dma_start(
                             out=dst[:, ci * width:(ci + 1) * width],
                             in_=kcache.ap()[row0:row0 + 128,
                                             base + ci * width:base + (ci + 1) * width]),
                         inc=cls, amt=16, wrpre=(key,), gate=(nld < 2))
                    lnx_w.append((cls, key))
                    nld += 1
            p.op("y", lambda e, row0=row0: e.dma_start(
                out=lnx[:, :], in_=lncache.ap()[row0:row0 + 128, :]),
                inc="dL", amt=16, wrpre=("lnx",), gate=False)
            vL, vL2 = p.val("dL"), p.val("dL2")
            for cls, key in lnx_w:
                p.hz(key).note_wr(cls, vL if cls == "dL" else vL2)
            p.hz("lnx").note_wr("dL", vL)
            for hp in range(H // 2):
                mms = [lambda e, h, hp=hp, t_=t_: e.transpose(
                    h[:, t_ * 64:(t_ + 1) * 64].bitcast(BF16),
                    kpt[:, (hp * 4 + t_) * 128:(hp * 4 + t_ + 1) * 128],
                    identb[:, :]) for t_ in range(4)]
                sl, tv = fill(p, pool, mms, rd=(f"kptc{hp // 2}", "identb"))
                kw = dict(wrpre=(f"ekpc{hp // 2}",)) if hp % 2 == 0 else \
                    dict(wrpost=(f"ekpc{hp // 2}",))
                evac(p, "v", sl, tv, lambda e, h, hp=hp: e.tensor_copy(
                    ekp[:, hp * 512:(hp + 1) * 512], h[:, 0:256].bitcast(BF16)), **kw)
            def at_am(hp):
                mms = []
                for hh in range(2):
                    for j in range(2):
                        h_ = hp * 2 + hh
                        mms.append(lambda e, h, h_=h_, hh=hh, j=j: e.matmul(
                            h[:, hh * 128:(hh + 1) * 128],
                            ekp[:, (h_ * 2 + j) * 128:(h_ * 2 + j + 1) * 128],
                            eqp[:, (h_ * 2 + j) * 128:(h_ * 2 + j + 1) * 128],
                            start=(hh == 0 and j == 0), stop=(hh == 1 and j == 1),
                            skip_group_check=True))
                sl_a, tva = fill(p, pool, mms,
                                 rd=(f"ekpc{hp // 2}", f"eqpc{hp // 2}"))
                slot = hp % 2
                mt3 = masktri[:, :].rearrange("p (one f) -> p one f", one=1) \
                    .broadcast_to([128, 2, 128])
                evac(p, "v", sl_a, tva, lambda e, h, slot=slot, mt3=mt3: e.tensor_tensor(
                    amb[slot][:, :].rearrange("p (g f) -> p g f", g=2),
                    h[:, 0:256].rearrange("p (g f) -> p g f", g=2), mt3, ALU.mult),
                    rd=("masktri",), wr=(f"amb{slot}",))

            at_am(0)
            for h_ in range(H):
                c0 = h_ * (2 * D + 2)
                hp = h_ // 2
                hh = h_ % 2
                slot = hp % 2
                if hh == 0 and hp + 1 < H // 2:
                    at_am(hp + 1)
                mms = []
                for j in range(2):
                    mms.append(lambda e, h, h_=h_, j=j, c0=c0: e.matmul(
                        h[:, 0:64], eqp[:, (h_ * 2 + j) * 128:(h_ * 2 + j + 1) * 128],
                        stateb[:, c0 + j * 64:c0 + (j + 1) * 64],
                        start=(j == 0), stop=False))
                mms.append(lambda e, h, h_=h_, slot=slot, hh=hh: e.matmul(
                    h[:, 0:64], amb[slot][:, hh * 128:(hh + 1) * 128],
                    vtm[:, h_ * D:(h_ + 1) * D],
                    start=False, stop=True))
                for j in range(2):
                    mms.append(lambda e, h, h_=h_, j=j, c0=c0: e.matmul(
                        h[:, 64:65], eqp[:, (h_ * 2 + j) * 128:(h_ * 2 + j + 1) * 128],
                        stateb[:, c0 + 128 + j:c0 + 129 + j],
                        start=(j == 0), stop=False))
                mms.append(lambda e, h, h_=h_, slot=slot, hh=hh: e.matmul(
                    h[:, 64:65], amb[slot][:, hh * 128:(hh + 1) * 128], onescb[:, :],
                    start=False, stop=True))
                sln, tvn = fill(p, pool, mms,
                                rd=(f"stb{h_}", f"amb{slot}", f"vtmc{h_ // 4}",
                                    "onescb", f"eqpc{h_ // 4}"))
                rslot = h_ % 2
                p.op("v", lambda e, h=sln["h"], rslot=rslot: e.reciprocal(
                    rcol[:, rslot:rslot + 1], h[:, 64:65]),
                    inc="sV", wr=(f"rcol{rslot}",), waits=[("sT", tvn)])
                kw = dict(wrpre=("atm",)) if h_ == 0 else {}
                if h_ == H - 1:
                    kw = dict(wr=("atm",))
                evac(p, "s", sln, tvn, lambda e, h, h_=h_, rslot=rslot: e.activation(
                    atm[:, h_ * D:(h_ + 1) * D], h[:, 0:64], AF.Copy,
                    scale=rcol[:, rslot:rslot + 1]),
                    rd=(f"rcol{rslot}",), **kw)
                emit_state_update(h_, phase_c=True)
            mms = [lambda e, h, k=k: e.transpose(
                h[:, k * 64:(k + 1) * 64].bitcast(BF16),
                atm[:, k * 128:(k + 1) * 128], identb[:, :]) for k in range(KT)]
            sl, tv = fill(p, pool, mms, rd=("atm", "identb"))
            evac(p, "v", sl, tv, lambda e, h: e.tensor_copy(
                afm[:, :], h[:, :].bitcast(BF16)), wr=("afm",))
            for half in range(2):
                mms = []
                for k in range(KT):
                    mms.append(lambda e, h, k=k, half=half: e.matmul(
                        h[:, :], afm[:, k * 128:(k + 1) * 128],
                        wob[:, k * DM + half * 512:k * DM + (half + 1) * 512],
                        start=(k == 0), stop=False))
                mms.append(lambda e, h, half=half: e.matmul(
                    h[:, :], onesb[0:1, :], obr[0:1, half * 512:(half + 1) * 512],
                    start=False, stop=True))
                sl, tv = fill(p, pool, mms, rd=("afm", "wob", "onesb", "obr"))
                kwa = dict(wrpre=("scr2",)) if half == 0 else dict(wrpost=("scr2",))
                zv = p.op("v", lambda e, half=half, s=s, par=par: e.tensor_scalar(
                    scr2[:, half * 512:(half + 1) * 512],
                    x0g[par][:, s * DM + half * 512:s * DM + (half + 1) * 512],
                    lnx[:, 0:1], lnx[:, 1:2], ALU.add, ALU.mult),
                    inc="sV", rd=(f"x0g{par}", "lnx"), **kwa)
                kwc = dict(wrpre=("resf",)) if half == 0 else dict(wrpost=("resf",))
                evac(p, "v", sl, tv, lambda e, h, half=half: e.tensor_tensor(
                    resf[:, half * 512:(half + 1) * 512],
                    scr2[:, half * 512:(half + 1) * 512], h[:, :], ALU.add),
                    waits=[("sV", zv)], **kwc)
            emit_ln_stats("resf", resf[:, :], LNE2)
            p.op("v", lambda e: e.tensor_scalar(scr2[:, :], resf[:, :],
                                                stat[:, 47:48], stat[:, 48:49],
                                                ALU.add, ALU.mult),
                 inc="sV", rd=("resf", "stat2", "stat2s"), wr=("scr2",))
            p.op("v", lambda e: e.tensor_tensor(scr2[:, :], scr2[:, :], gammaB[:, :],
                                                ALU.mult),
                 inc="sV", rd=("scr2", "gammaB"), wr=("scr2",))
            p.op("v", lambda e: e.tensor_tensor(scr2[:, :], scr2[:, :], betaB[:, :],
                                                ALU.add),
                 inc="sV", rd=("scr2", "betaB"), wr=("scr2",))
            p.op("v", lambda e, s=s, par=par: e.tensor_tensor(
                resf[:, :], scr2[:, :], x0g[par][:, s * DM:(s + 1) * DM], ALU.add),
                inc="sV", rd=("scr2", f"x0g{par}"), wr=("resf",))
            x1v = p.op("g", lambda e, g=g, s=s: e.dma_start(
                out=x1scr.ap()[g * GRP + s * 128:g * GRP + (s + 1) * 128, :],
                in_=resf[:, :]), inc="dSt", amt=16, rd=("resf",))
            x1_store_vals.append(x1v)

    p.phase = "D"
    p.barrier()
    r2b.close()
    r2a.close()
    w12b = [mk(r3, f"w12b{i}", [128, KT * HID], BF16) for i in range(2)]
    w3b = [mk(r3, "w3b0", [128, 16 * DM], BF16)] * 2
    xn3f = mk(r3, "xn3f", [128, KT * MTOK], BF16)
    accf = mk(r3, "accf", [128, MSUB * DM], F32)
    a_fm = mk(r3, "a_fm", [128, 16 * MTOK], BF16)
    cwf = mk(r3, "cwf", [128, MSUB * NE], F32)
    cwfb = mk(r3, "cwfb", [128, MSUB * NE], BF16)
    cwfm = mk(r3, "cwfm", [NE, MSUB * 128], BF16)
    logitsb = mk(r3, "logitsb", [128, MSUB * NE], F32)
    usel = mk(r3, "usel", [128, MSUB * NE], F32)
    uexp = mk(r3, "uexp", [128, MSUB * NE], F32)
    l3m = mk(r3, "l3m", [128, 8], F32)
    l3r = mk(r3, "l3r", [128, 8], F32)
    z3bA = [mk(r3, f"z3b{i}", [128, DM], BF16) for i in range(2)]

    def wload(e_, buf):
        for k in range(KT):
            kw = dict(wrpre=(f"w12b{buf}",)) if k == 0 else {}
            if k == KT - 1:
                kw["wrpost"] = (f"w12b{buf}",)
            p.op("y", lambda e, e_=e_, k=k, buf=buf: e.dma_start(
                out=w12b[buf][:, k * HID:(k + 1) * HID],
                in_=ccout_w.ap()[e_ * WROWS + k * 128:e_ * WROWS + (k + 1) * 128, :]),
                inc="dW", amt=16,
                waits=[("gC", 1)] if k == 0 else [], **kw)
        for k in range(16):
            r0 = e_ * WROWS + DM + 1 + k * 64
            kw = dict(wrpre=("w3b",)) if k == 0 else {}
            if k == 15:
                kw["wrpost"] = ("w3b",)
            p.op("y", lambda e, r0=r0, k=k, buf=buf: e.dma_start(
                out=w3b[buf][:, k * DM:(k + 1) * DM],
                in_=ccout_w.ap()[r0:r0 + 64, :].rearrange("a (b c) -> (a b) c", b=2)),
                inc="dW", amt=16, **kw)
        return p.val("dW")

    wload(0, 0)
    def x1_load(mg, s):
        p.op("y", lambda e, mg=mg, s=s: e.dma_start(
            out=accf[:, s * DM:(s + 1) * DM],
            in_=x1scr.ap()[mg * MTOK + s * 128:mg * MTOK + (s + 1) * 128, :]),
            inc="dX1", amt=16, wrpre=(f"accf{s}",), gate=(s == 0),
            waits=[("dSt", x1_store_vals[mg * MSUB + s])])

    for mg in range(MG):
        if mg == 0:
            for s in range(MSUB):
                x1_load(0, s)
        xend = p.val("dX1")
        for s in range(MSUB):
            p.hz(f"accf{s}").note_wr("dX1", xend)
        emit_ln_stats_group(
            [(f"accf{s}", accf[:, s * DM:(s + 1) * DM]) for s in range(MSUB)],
            LNE1, l3m, l3r, "l3g")
        for s in range(MSUB):
            c0 = s * DM
            zb3, zk3 = z3bA[s % 2], f"z3b{s % 2}"
            p.op("v", lambda e, c0=c0, s=s, zb3=zb3: e.tensor_scalar(
                zb3[:, :], accf[:, c0:c0 + DM], l3m[:, s:s + 1], l3r[:, s:s + 1],
                ALU.add, ALU.mult),
                inc="sV", rd=(f"accf{s}", "l3g"), wr=(zk3,))
            mms = [lambda e, h, k=k, zb3=zb3: e.transpose(
                h[:, k * 64:(k + 1) * 64].bitcast(BF16),
                zb3[:, k * 128:(k + 1) * 128], identb[:, :]) for k in range(KT)]
            sl, tv = fill(p, pool, mms, rd=(zk3, "identb"))
            dst3 = xn3f[:, :].rearrange("p (k g) -> p k g", k=KT)[:, :, s * 128:(s + 1) * 128]
            evac(p, "v", sl, tv, lambda e, h, dst3=dst3: e.tensor_copy(
                dst3, h[:, :].bitcast(BF16).rearrange("p (k g) -> p k g", k=KT)),
                wr=(f"xn3f{s}",))
            mms = []
            for k in range(KT):
                mms.append(lambda e, h, k=k, s=s: e.matmul(
                    h[:, 0:NE], xn3f[:, k * MTOK + s * 128:k * MTOK + (s + 1) * 128],
                    wrb[:, k * NE:(k + 1) * NE], start=(k == 0), stop=False))
            mms.append(lambda e, h: e.matmul(
                h[:, 0:NE], onesb[0:1, :], brb[0:1, :], start=False, stop=True))
            sl, tv = fill(p, pool, mms, rd=(f"xn3f{s}", "wrb", "brb", "onesb"))
            evac(p, "v", sl, tv, lambda e, h, s=s: e.tensor_copy(
                logitsb[:, s * NE:(s + 1) * NE], h[:, 0:NE]),
                wrpre=("lgt",) if s == 0 else (),
                wrpost=("lgt",) if s == MSUB - 1 else ())
        nsub = MSUB
        u3 = lambda t: t[:, :].rearrange("p (s e) -> p s e", s=nsub)
        g3 = lambda c: stat[:, c:c + nsub].rearrange(
            "p (s one) -> p s one", one=1)
        g3b = lambda c: stat[:, c:c + nsub].rearrange(
            "p (s one) -> p s one", one=1).broadcast_to([128, nsub, NE])
        p.op("s", lambda e: e.activation(uexp[:, :], logitsb[:, :], AF.Exp),
             inc="sS", rd=("lgt",), wr=("uexp",))
        p.op("v", lambda e: e.tensor_reduce(g3(48), u3(uexp), AX.X, ALU.max),
             inc="sV", rd=("uexp",), wr=("rtA",))
        p.op("v", lambda e: e.tensor_tensor(u3(usel), u3(uexp), g3b(48), ALU.is_ge),
             inc="sV", rd=("uexp", "rtA"), wr=("usel",))
        p.op("v", lambda e: e.tensor_scalar(usel[:, :], usel[:, :], -1.0, 1.0,
                                            ALU.mult, ALU.add),
             inc="sV", rd=("usel",), wr=("usel",))
        p.op("v", lambda e: e.tensor_tensor(usel[:, :], uexp[:, :], usel[:, :],
                                            ALU.mult),
             inc="sV", rd=("usel", "uexp"), wr=("usel",))
        p.op("v", lambda e: e.tensor_reduce(g3(48), u3(usel), AX.X, ALU.max),
             inc="sV", rd=("usel",), wr=("rtA",))
        p.op("v", lambda e: e.tensor_tensor(u3(usel), u3(uexp), g3b(48), ALU.is_ge),
             inc="sV", rd=("uexp", "rtA"), wr=("usel",))
        p.op("v", lambda e: e.tensor_tensor(usel[:, :], uexp[:, :], usel[:, :],
                                            ALU.mult),
             inc="sV", rd=("usel", "uexp"), wr=("usel",))
        p.op("v", lambda e: e.tensor_reduce(g3(48), u3(usel), AX.X, ALU.add),
             inc="sV", rd=("usel",), wr=("rtA",))
        p.op("v", lambda e: e.reciprocal(stat[:, 56:56 + nsub], stat[:, 48:48 + nsub]),
             inc="sV", rd=("rtA",), wr=("rtB",))
        p.op("v", lambda e: e.tensor_tensor(u3(cwf), u3(usel), g3b(56), ALU.mult),
             inc="sV", rd=("usel", "rtB"), wr=("cwfall",))
        p.op("v", lambda e: e.tensor_copy(cwfb[:, :], cwf[:, :]),
             inc="sV", rd=("cwfall",), wr=("cwfb",))
        for s in range(MSUB):
            c0 = s * DM
            sl, tv = fill(p, pool, [lambda e, h, s=s: e.transpose(
                h[0:NE, 0:64].bitcast(BF16), cwfb[:, s * NE:(s + 1) * NE],
                identb[:, :])],
                rd=("cwfb", "identb"))
            evac(p, "v", sl, tv, lambda e, h, s=s: e.tensor_copy(
                cwfm[:, s * 128:(s + 1) * 128], h[0:NE, 0:64].bitcast(BF16)),
                wr=(f"cwfm{s}",))
            for half in range(2):
                sl, tv = fill(p, pool, [lambda e, h, s=s, half=half: e.matmul(
                    h[:, :], cwfm[0:NE, s * 128:(s + 1) * 128],
                    b3b[0:NE, half * 512:(half + 1) * 512], start=True, stop=True)],
                    rd=(f"cwfm{s}", "b3b"))
                evac(p, "v", sl, tv, lambda e, h, c0=c0, half=half: e.tensor_tensor(
                    accf[:, c0 + half * 512:c0 + (half + 1) * 512],
                    accf[:, c0 + half * 512:c0 + (half + 1) * 512], h[:, :], ALU.add),
                    rd=(f"accf{s}",), wr=(f"accf{s}",))
        for e_ in range(NE):
            gi = mg * NE + e_
            buf = gi % 2
            nhalf = max(1, MTOK // 512)
            wfrac = min(512, MTOK)
            for o in range(16):
                for hh in range(nhalf):
                    mms = []
                    for k in range(KT):
                        mms.append(lambda e, h, o=o, hh=hh, k=k, buf=buf: e.matmul(
                            h[:, 0:wfrac],
                            w12b[buf][:, k * HID + o * 128:k * HID + (o + 1) * 128],
                            xn3f[:, k * MTOK + hh * 512:k * MTOK + hh * 512 + wfrac],
                            start=(k == 0), stop=(k == KT - 1)))
                    sl, tv = fill(p, pool, mms, rd=(f"w12b{buf}",) + tuple(
                        f"xn3f{s2}" for s2 in range(MSUB)))
                    kw = {}
                    if o == 0 and hh == 0:
                        kw["wrpre"] = ("a_fm",)
                    if o == 15 and hh == nhalf - 1:
                        kw["wrpost"] = ("a_fm",)
                    evac(p, "s", sl, tv, lambda e, h, o=o, hh=hh, e_=e_: e.activation(
                        a_fm[:, o * MTOK + hh * 512:o * MTOK + hh * 512 + wfrac],
                        h[:, 0:wfrac], AF.Silu,
                        bias=b12f[:, e_ * 16 + o:e_ * 16 + o + 1]),
                        rd=("b12f",), **kw)
            for s in range(MSUB):
                for half in range(2):
                    mms = []
                    for k in range(16):
                        mms.append(lambda e, h, s=s, half=half, k=k, buf=buf: e.matmul(
                            h[:, :], a_fm[:, k * MTOK + s * 128:k * MTOK + (s + 1) * 128],
                            w3b[buf][:, k * DM + half * 512:k * DM + (half + 1) * 512],
                            start=(k == 0), stop=(k == 15)))
                    sl, tv = fill(p, pool, mms, rd=("a_fm", "w3b"))
                    evac(p, "v", sl, tv, lambda e, h, s=s, half=half, e_=e_: e.scalar_tensor_tensor(
                        accf[:, s * DM + half * 512:s * DM + (half + 1) * 512],
                        h[:, :], cwf[:, s * NE + e_:s * NE + e_ + 1],
                        accf[:, s * DM + half * 512:s * DM + (half + 1) * 512],
                        ALU.mult, ALU.add),
                        rd=("cwfall", f"accf{s}"), wr=(f"accf{s}",))
                    if e_ == NE - 1 and half == 1:
                        p.op("y", lambda e, mg=mg, s=s: e.dma_start(
                            out=outd[mg * MTOK + s * 128:mg * MTOK + (s + 1) * 128, :],
                            in_=accf[:, s * DM:(s + 1) * DM]),
                            inc="dO", amt=16, rd=(f"accf{s}",))
                        if mg + 1 < MG:
                            x1_load(mg + 1, s)
            if gi + 1 < MG * NE:
                wload((gi + 1) % NE, (gi + 1) % 2)
    out_done = p.val("dO")

    p.op("y", lambda e: e.nop(), inc=None, waits=[("dO", out_done)])
    p.op("t", lambda e: e.nop(), inc=None, waits=[("dO", out_done)])

    r3.close()
    n_ops = sum(len(v) for v in p.ops.values())
    print(f"[build] ops: " + ", ".join(f"{k}:{len(v)}" for k, v in p.ops.items()) +
          f" total={n_ops}; sbuf left/part: {nc.sbuf_bytes_remaining}")

    def play(eng):
        def body(e):
            ops, phs = p.ops[eng], p.phases[eng]
            i = 0
            while i < len(ops):
                j = i
                while j < len(ops) and phs[j] == phs[i]:
                    j += 1
                with nc.named_scope(phs[i]):
                    for k in range(i, j):
                        ops[k](e)
                i = j
        return body

    with nc.Block() as block:
        block.tensor(play("t"))
        block.vector(play("v"))
        block.scalar(play("s"))
        block.gpsimd(play("g"))
        block.sync(play("y"))

    return nc


def make_in_maps(inputs, tpc=4096):
    g = lambda k: np.ascontiguousarray(np.asarray(inputs[k], dtype=np.float32))
    x = np.asarray(inputs["L0_attn_x"], dtype=np.float32)
    B, S, _ = x.shape
    qpb = S // tpc
    maps = []
    for cid in range(NCORES):
        b, q = cid // qpb, cid % qpb
        mask = np.zeros(NCORES, np.float32)
        for r in range(NCORES):
            rb, rq = r // qpb, r % qpb
            if rb == b and rq < q:
                mask[r] = 1.0
        maps.append(dict(
            x0=np.ascontiguousarray(x[b, q * tpc:(q + 1) * tpc, :]),
            qw=g("L0_attn_q_w"), kw=g("L0_attn_k_w"), vw=g("L0_attn_v_w"),
            ow=g("L0_attn_out_w"),
            qb=g("L0_attn_q_b"), kb=g("L0_attn_k_b"), vb=g("L0_attn_v_b"),
            ob=g("L0_attn_out_b"),
            proj=g("L0_attn_proj"), gamma=g("L0_attn_gamma"), beta=g("L0_attn_beta"),
            Wr=g("L0_ffn_Wr"), brr=g("L0_ffn_br"), b3=g("L0_ffn_b3"),
            w1e=np.ascontiguousarray(g("L0_ffn_W1")[cid]),
            w2e=np.ascontiguousarray(g("L0_ffn_W2")[cid]),
            b1e=np.ascontiguousarray(g("L0_ffn_b1")[cid]),
            b2e=np.ascontiguousarray(g("L0_ffn_b2")[cid]),
            w3e=np.ascontiguousarray(g("L0_ffn_W3")[cid]),
            pmask=np.repeat(mask, 128).astype(np.float32),
        ))
    return maps


def kernel(**inputs):
    tpc = 4096
    nc = build_program(tpc=tpc, spg=4)
    in_maps = make_in_maps(inputs, tpc=tpc)
    res = run_bass_kernel_spmd(nc, in_maps, core_ids=list(range(NCORES)))
    x = np.asarray(inputs["L0_attn_x"])
    B, S, _ = x.shape
    qpb = S // tpc
    out = np.empty((B, S, DM), np.float32)
    for cid in range(NCORES):
        b, q = cid // qpb, cid % qpb
        out[b, q * tpc:(q + 1) * tpc, :] = res.results[cid]["out"]
    return out
